# version 12
# baseline (speedup 1.0000x reference)
import numpy as np
from contextlib import ExitStack

import concourse.bass as bass
import concourse.mybir as mybir
from concourse.bass_utils import run_bass_kernel_spmd

F32 = mybir.dt.float32
BF16 = mybir.dt.bfloat16
AF = mybir.ActivationFunctionType
ALU = mybir.AluOpType
AX = mybir.AxisListType

P = 128
D = 1024
DT = D // P
H = 4096
HT = H // P
SEQ = 2048
NB = 4
NCORES = 8
ACH = 512
FCH = 1024
NACH = SEQ // ACH
NFCH = SEQ // FCH
MASKVAL = -320000.0
ISD = 1.0 / 32.0
EPS = 1e-5
KiB = 1024


def ts(i, n):
    return slice(i * n, (i + 1) * n)


class Sem:
    def __init__(self, handle, name):
        self.h = handle
        self.name = name
        self.v = 0


class Plan:
    ENGINES = ("PE", "ACT", "DVE", "SP", "PL")

    def __init__(self):
        self.E = {k: [] for k in self.ENGINES}
        self.waited = {k: {} for k in self.ENGINES}

    def wait(self, en, sem, val):
        if val is None or val <= 0:
            return
        if self.waited[en].get(sem.name, 0) >= val:
            return
        self.waited[en][sem.name] = val
        self.E[en].append(lambda eng, s=sem.h, v=val: eng.wait_ge(s, v))

    def op(self, en, fn, inc=None):
        if inc is not None:
            sem, n = inc
            sem.v += n
            tgt = sem.v
            self.E[en].append(
                lambda eng, fn=fn, s=sem.h, n=n: fn(eng).then_inc(s, n))
            return tgt
        self.E[en].append(lambda eng, fn=fn: fn(eng))
        return None


class PBank:
    def __init__(self, ap, bf=None):
        self.ap = ap
        self.bf = bf
        self.free = None

    def acquire(self, plan, en="PE"):
        if self.free is not None:
            plan.wait(en, *self.free)


def build_program():
    nc = bass.Bass('TRN2', target_bir_lowering=False)

    xTd = nc.dram_tensor("xT", [NB, DT, P, SEQ], F32, kind="ExternalInput")
    outd = nc.dram_tensor("outT", [NB, DT, P, SEQ], F32, kind="ExternalOutput")
    x2d = nc.dram_tensor("x2scr", [NB, DT, P, SEQ], F32, kind="Internal")
    zd = nc.dram_tensor("zscr", [NB, DT, P, SEQ], BF16, kind="Internal")
    z3d = nc.dram_tensor("z3scr", [NB, DT, P, SEQ], BF16, kind="Internal")

    wq_d = nc.dram_tensor("wq", [P, DT * DT * P], F32, kind="ExternalInput")
    wk_d = nc.dram_tensor("wk", [P, DT * DT * P], F32, kind="ExternalInput")
    wo_d = nc.dram_tensor("wo", [P, DT * DT * P], F32, kind="ExternalInput")
    wv_d = nc.dram_tensor("wv", [P, DT * D], F32, kind="ExternalInput")
    w1_d = nc.dram_tensor("w1p", [HT, P, DT * P], F32, kind="ExternalInput")
    w2_d = nc.dram_tensor("w2p", [HT, P, DT * P], F32, kind="ExternalInput")
    h1_d = nc.dram_tensor("h1p", [HT, P, HT * P], F32, kind="ExternalInput")
    h2_d = nc.dram_tensor("h2p", [HT, P, HT * P], F32, kind="ExternalInput")
    w3_d = nc.dram_tensor("w3p", [DT, P, HT * P], F32, kind="ExternalInput")
    bq_d = nc.dram_tensor("bq", [P, DT], F32, kind="ExternalInput")
    bk_d = nc.dram_tensor("bk", [P, DT], F32, kind="ExternalInput")
    bo_d = nc.dram_tensor("bo", [P, DT], F32, kind="ExternalInput")
    bvb_d = nc.dram_tensor("bvb", [1, D], F32, kind="ExternalInput")
    s1_d = nc.dram_tensor("s1c", [P, HT], F32, kind="ExternalInput")
    s2_d = nc.dram_tensor("s2c", [P, HT], F32, kind="ExternalInput")
    h1s_d = nc.dram_tensor("h1sc", [P, HT], F32, kind="ExternalInput")
    h2s_d = nc.dram_tensor("h2sc", [P, HT], F32, kind="ExternalInput")
    w3s_d = nc.dram_tensor("w3sc", [P, DT], F32, kind="ExternalInput")
    mask_d = nc.dram_tensor("mask", [P, NACH * ACH], F32, kind="ExternalInput")
    iden_d = nc.dram_tensor("iden", [P, P], BF16, kind="ExternalInput")
    onesc_d = nc.dram_tensor("onesc", [P, 1], BF16, kind="ExternalInput")
    onesr_d = nc.dram_tensor("onesr", [1, P], F32, kind="ExternalInput")

    plan = Plan()
    PEo = lambda fn, inc=None: plan.op("PE", fn, inc)
    ACTo = lambda fn, inc=None: plan.op("ACT", fn, inc)
    DVEo = lambda fn, inc=None: plan.op("DVE", fn, inc)
    SPo = lambda fn, inc=None: plan.op("SP", fn, inc)
    PLo = lambda fn, inc=None: plan.op("PL", fn, inc)

    mksem = lambda n: Sem(nc.alloc_semaphore(name=n), n)
    qW = mksem("qW")
    qX = mksem("qX")
    qS = mksem("qS")
    cP = mksem("cP")
    cA = mksem("cA")
    cD = mksem("cD")

    sb = lambda name, shape, dtype: nc.alloc_sbuf_tensor(name, shape, dtype)
    iden_s = sb("iden_s", [P, P], BF16)
    onesc_s = sb("onesc_s", [P, 1], BF16)
    onesr_s = sb("onesr_s", [1, P], F32)
    eps_s = sb("eps_s", [1, 1], F32)
    bq_s = sb("bq_s", [P, DT], F32)
    bk_s = sb("bk_s", [P, DT], F32)
    bo_s = sb("bo_s", [P, DT], F32)
    bvb_s = sb("bvb_s", [1, D], F32)
    s1_s = sb("s1_s", [P, HT], F32)
    s2_s = sb("s2_s", [P, HT], F32)
    h1s_s = sb("h1s_s", [P, HT], F32)
    h2s_s = sb("h2s_s", [P, HT], F32)
    w3s_s = sb("w3s_s", [P, DT], F32)
    mask_s = sb("mask_s", [P, NACH * ACH], BF16)
    mrow = sb("mrow", [1, FCH], F32)
    vrow = sb("vrow", [1, FCH], F32)
    rrow = sb("rrow", [1, FCH], F32)
    arow = vrow
    nmx = sb("nmx", [P, 1], F32)
    nmx2 = sb("nmx2", [P, 1], F32)
    esum = sb("esum", [P, 1], F32)
    rcol = sb("rcol", [P, 1], F32)
    marker = nc.alloc_sbuf_tensor("arena_marker", [P, 32], mybir.dt.uint8)
    base = nc.lookup_mloc(marker).addr

    def at(name, shape, dtype, off):
        return nc.alloc_sbuf_tensor_at(name, shape, dtype, offset=base + off)

    wq_s = at("wq_s", [P, DT * DT * P], BF16, 0)
    wo_s = at("wo_s", [P, DT * DT * P], BF16, 16 * KiB)
    kT = at("kT", [P, DT * SEQ], BF16, 32 * KiB)
    vtok = at("vtok", [P, (SEQ // P) * D], BF16, 64 * KiB)
    xc = at("xc", [P, DT * ACH], F32, 96 * KiB)
    xbp = at("xbp", [P, DT * ACH], BF16, 112 * KiB)
    xsq = at("xsq", [P, DT * ACH], BF16, 120 * KiB)
    zc = at("zc", [P, DT * ACH], BF16, 128 * KiB)
    t1 = at("t1", [P, 512], F32, 136 * KiB)
    t2 = at("t2", [P, 512], F32, 138 * KiB)
    wk_s = at("wk_s", [P, DT * DT * P], BF16, 140 * KiB)
    wv_s = at("wv_s", [P, DT * D], BF16, 156 * KiB)
    qc_s = at("qc_s", [P, DT * ACH], BF16, 140 * KiB)
    srow = at("srow", [P, SEQ], F32, 148 * KiB)
    ptb = at("ptb", [P, (SEQ // P) * ACH], BF16, 156 * KiB)
    srow2 = at("srow2", [P, SEQ], F32, 96 * KiB)
    z3c = at("z3c", [P, DT * ACH], BF16, 96 * KiB)
    A_END = 172 * KiB

    abuf = at("abuf", [P, HT * FCH], BF16, 0)
    x2c2 = at("x2c2", [P, DT * FCH], F32, 0)
    fb16 = at("fb16", [P, DT * FCH], BF16, 96 * KiB)
    fsq16 = at("fsq16", [P, DT * FCH], BF16, 112 * KiB)
    bbuf = at("bbuf", [P, HT * FCH], BF16, 64 * KiB)
    fbuf = at("fbuf", [P, DT * FCH], F32, 64 * KiB)
    z3 = at("z3", [P, DT * FCH], BF16, 128 * KiB)
    wsl = [at(f"wsl{i}", [P, DT * P], BF16, 144 * KiB + i * 2 * KiB)
           for i in range(4)]
    hsl = [at(f"hsl{i}", [P, HT * P], BF16, 152 * KiB + i * 8 * KiB)
           for i in range(2)]
    t3 = at("t3", [P, 512], F32, 180 * KiB)
    t4 = at("t4", [P, 512], F32, 182 * KiB)
    u_s = [at(f"u{i}", [P, 512], BF16, 172 * KiB + i * KiB)
           for i in range(4)]
    v2_s = [at(f"v2{i}", [P, 512], BF16, 176 * KiB + i * KiB)
            for i in range(4)]
    F_END = 184 * KiB
    assert base + max(A_END, F_END) <= nc.sbuf_top, (
        base, A_END, F_END, nc.sbuf_top)

    pb_t = [nc.alloc_psum_tensor(f"pb{i}", [P, 512], F32) for i in range(4)]
    pR_t = nc.alloc_psum_tensor("pR", [P, 1024], F32)
    pA_t = nc.alloc_psum_tensor("pA", [P, 1024], F32)
    pbanks = [PBank(h[:, :]) for h in pb_t]
    pA_bf = pA_t.ap().bitcast(BF16)
    pRl = PBank(pR_t[:, 0:512])
    pRh = PBank(pR_t[:, 512:1024])
    pAl = PBank(pA_t[:, 0:512], pA_bf[:, 0:1024])
    pAh = PBank(pA_t[:, 512:1024], pA_bf[:, 1024:2048])
    rot = [0]

    def next_bank():
        bk = pbanks[rot[0] % 4]
        rot[0] += 1
        return bk

    PLo(lambda g: g.memset(eps_s[0:1, 0:1], EPS), inc=(qW, 1))
    for dst, src in ((iden_s, iden_d), (onesc_s, onesc_d), (onesr_s, onesr_d),
                     (bq_s, bq_d), (bk_s, bk_d), (bo_s, bo_d), (bvb_s, bvb_d),
                     (s1_s, s1_d), (s2_s, s2_d), (h1s_s, h1s_d),
                     (h2s_s, h2s_d), (w3s_s, w3s_d), (mask_s, mask_d)):
        PLo(lambda g, d=dst, s=src: g.dma_start(d[:, :], s.ap()), inc=(qW, 16))
    for en in ("PE", "ACT", "DVE", "SP"):
        plan.wait(en, qW, qW.v)

    def emit_ln_p1a(width, xb_h, xq_h, ready_act):
        nw = width // 512
        plan.wait("PE", cA, ready_act)
        stat = [next_bank() for _ in range(2 * nw)]
        pv = None
        for j in range(nw):
            for idx, sh in ((0, xb_h), (1, xq_h)):
                bk = stat[2 * j + idx]
                bk.acquire(plan)
                for t in range(DT):
                    pv = PEo(lambda e, j=j, t=t, bk=bk, sh=sh: e.matmul(
                        bk.ap[0:1, :], onesc_s[:, 0:1],
                        sh[:, t * width + j * 512: t * width + (j + 1) * 512],
                        start=(t == 0), stop=(t == DT - 1)),
                        inc=(cP, 1) if t == DT - 1 else None)
        plan.wait("DVE", cP, pv)
        dv = None
        for j in range(nw):
            DVEo(lambda e, j=j, bk=stat[2 * j]: e.tensor_scalar(
                mrow[0:1, ts(j, 512)], bk.ap[0:1, :], 1.0 / D, None, ALU.mult))
            DVEo(lambda e, j=j: e.tensor_mul(
                vrow[0:1, ts(j, 512)], mrow[0:1, ts(j, 512)],
                mrow[0:1, ts(j, 512)]))
            dv = DVEo(lambda e, j=j, bk=stat[2 * j + 1]: e.scalar_tensor_tensor(
                vrow[0:1, ts(j, 512)], bk.ap[0:1, :], 1.0 / D,
                vrow[0:1, ts(j, 512)], ALU.mult, ALU.subtract), inc=(cD, 1))
        for s in stat:
            s.free = (cD, dv)
        plan.wait("ACT", cD, dv)
        ACTo(lambda e: e.activation(rrow[0:1, 0:width], vrow[0:1, 0:width],
                                    AF.Ln, bias=eps_s[0:1, 0:1]))
        av = ACTo(lambda e: e.activation(rrow[0:1, 0:width], rrow[0:1, 0:width],
                                         AF.Exp, scale=-0.5), inc=(cA, 1))
        plan.wait("DVE", cA, av)
        dv = DVEo(lambda e: e.scalar_tensor_tensor(
            arow[0:1, 0:width], mrow[0:1, 0:width], -1.0, rrow[0:1, 0:width],
            ALU.mult, ALU.mult), inc=(cD, 1))
        return (nw, dv)

    def emit_ln_p1b(st_a):
        nw, dv = st_a
        plan.wait("PE", cD, dv)
        Rb = [pRl, pRh][:nw]
        Ab = [pAl, pAh][:nw]
        pv = None
        for j in range(nw):
            Rb[j].acquire(plan)
            pv = PEo(lambda e, j=j, bk=Rb[j]: e.matmul(
                bk.ap, onesr_s[0:1, :], rrow[0:1, ts(j, 512)],
                start=True, stop=True), inc=(cP, 1))
        for j in range(nw):
            Ab[j].acquire(plan)
            pv = PEo(lambda e, j=j, bk=Ab[j]: e.matmul(
                bk.ap, onesr_s[0:1, :], arow[0:1, ts(j, 512)],
                start=True, stop=True), inc=(cP, 1))
        return (nw, Rb, Ab, pv)

    def emit_ln_p1(width, xb_h, xq_h, ready_act):
        return emit_ln_p1b(emit_ln_p1a(width, xb_h, xq_h, ready_act))

    def emit_ln_p2(width, st, src_h, dst_h, apply_cb=None, tmp=None):
        nw, Rb, Ab, pv = st
        plan.wait("DVE", cP, pv)
        dv = None
        for t in range(DT):
            for j in range(nw):
                slz = slice(t * width + j * 512, t * width + (j + 1) * 512)
                last = (t == DT - 1 and j == nw - 1)
                if apply_cb is None:
                    DVEo(lambda e, slz=slz, j=j, tmp=tmp: e.tensor_mul(
                        tmp[:, 0:512], src_h[:, slz], Rb[j].ap))
                    dv = DVEo(lambda e, slz=slz, j=j, tmp=tmp: e.tensor_add(
                        dst_h[:, slz], tmp[:, 0:512], Ab[j].ap),
                        inc=(cD, 1) if last else None)
                else:
                    dv = apply_cb(slz, Rb[j].ap, Ab[j].ap, last)
        for j in range(nw):
            Rb[j].free = (cD, dv)
            Ab[j].free = (cD, dv)
        return dv

    def emit_ln(width, xb_h, xq_h, src_h, dst_h, ready_act, apply_cb=None,
                tmp=None):
        st = emit_ln_p1(width, xb_h, xq_h, ready_act)
        return emit_ln_p2(width, st, src_h, dst_h, apply_cb, tmp)

    def emit_proj(w_h, z_h, width, dst_fn, bias_col, func, w_ready, z_rdy_cD):
        plan.wait("PE", cD, z_rdy_cD)
        plan.wait("PE", qW, w_ready)
        av = None
        for mt in range(DT):
            bk = next_bank()
            bk.acquire(plan)
            pv = None
            for kt in range(DT):
                pv = PEo(lambda e, mt=mt, kt=kt, bk=bk: e.matmul(
                    bk.ap, w_h[:, (mt * DT + kt) * P:(mt * DT + kt + 1) * P],
                    z_h[:, ts(kt, width)], start=(kt == 0),
                    stop=(kt == DT - 1)), inc=(cP, 1) if kt == DT - 1 else None)
            plan.wait("ACT", cP, pv)
            if bias_col is not None:
                av = ACTo(lambda e, mt=mt, bk=bk: e.activation(
                    dst_fn(mt), bk.ap, func, bias=bias_col[:, mt:mt + 1]),
                    inc=(cA, 1))
            else:
                av = ACTo(lambda e, mt=mt, bk=bk: e.activation(
                    dst_fn(mt), bk.ap, func), inc=(cA, 1))
            bk.free = (cA, av)
        return av

    prev_gate = []
    dst_prev_users = {}
    attn_gate = []
    prevF_cd = 0
    prevF_qs = 0
    prevF_ln4 = None
    uflip = [0]

    for b in range(NB):
        for s2, v2 in attn_gate:
            plan.wait("PL", s2, v2)
        w_rdy = {}
        for nm, dsth, srcd in (("wk", wk_s, wk_d), ("wv", wv_s, wv_d),
                               ("wq", wq_s, wq_d), ("wo", wo_s, wo_d)):
            if nm == "wq":
                plan.wait("PL", cD, prevF_cd)
                plan.wait("PL", qS, prevF_qs)
            if id(dsth) in dst_prev_users:
                plan.wait("PL", *dst_prev_users[id(dsth)])
            w_rdy[nm] = PLo(lambda g, d=dsth, s=srcd: g.dma_start(
                d[:, :], s.ap()), inc=(qW, 16))

        plan.wait("ACT", cD, prevF_cd)
        plan.wait("ACT", qS, prevF_qs)

        xls = [None] * NACH
        casts = [None] * NACH
        z_store = [0] * (NACH + 1)

        def a1_load(tc):
            if tc == 0 and prevF_ln4 is not None:
                plan.wait("SP", cP, prevF_ln4[1])
                plan.wait("SP", cA, prevF_ln4[0])
            else:
                plan.wait("SP", cP, cP.v)
                plan.wait("SP", cA, cA.v)
                plan.wait("SP", cD, cD.v)
            sl = slice(tc * ACH, (tc + 1) * ACH)
            xls[tc] = SPo(lambda e, sl=sl, b=b: e.dma_start(
                xc[:, :].rearrange("p (t f) -> p t f", t=DT),
                xTd.ap()[b, :, :, sl].rearrange("t p f -> p t f")),
                inc=(qX, 16))

        def a1_casts(tc):
            plan.wait("ACT", qX, xls[tc])
            if tc == 0 and prevF_ln4 is not None:
                plan.wait("ACT", cP, prevF_ln4[1])
            else:
                plan.wait("ACT", cP, cP.v)
            ACTo(lambda e: e.activation(xbp[:, :], xc[:, :], AF.Copy))
            casts[tc] = ACTo(lambda e: e.activation(
                xsq[:, :], xc[:, :], AF.Square), inc=(cA, 1))

        a1_load(0)
        a1_casts(0)
        last_v_av = None
        zdones = [None] * NACH

        def emit_k(tc):
            emit_proj(wk_s, zc, ACH,
                      lambda mt, tc=tc: kT[:, mt * SEQ + tc * ACH:
                                           mt * SEQ + (tc + 1) * ACH],
                      bk_s, AF.Relu, w_rdy["wk"], zdones[tc])

        def emit_v(tc):
            nonlocal last_v_av
            plan.wait("PE", qW, w_rdy["wv"])
            for tt in range(ACH // P):
                for nch in range(2):
                    bk = next_bank()
                    bk.acquire(plan)
                    for kt in range(DT):
                        PEo(lambda e, tt=tt, nch=nch, kt=kt, bk=bk: e.matmul(
                            bk.ap,
                            zc[:, kt * ACH + tt * P: kt * ACH + (tt + 1) * P],
                            wv_s[:, kt * D + nch * 512:
                                 kt * D + (nch + 1) * 512],
                            start=(kt == 0), stop=False))
                    pv = PEo(lambda e, nch=nch, bk=bk: e.matmul(
                        bk.ap, onesr_s[0:1, :], bvb_s[0:1, ts(nch, 512)],
                        start=False, stop=True), inc=(cP, 1))
                    plan.wait("ACT", cP, pv)
                    gt = tc * (ACH // P) + tt
                    last_v_av = ACTo(lambda e, gt=gt, nch=nch, bk=bk:
                                     e.activation(
                        vtok[:, gt * D + nch * 512: gt * D + (nch + 1) * 512],
                        bk.ap, AF.Relu), inc=(cA, 1))
                    bk.free = (cA, last_v_av)

        for tc in range(NACH):
            sl = slice(tc * ACH, (tc + 1) * ACH)
            st_a = emit_ln_p1a(ACH, xbp, xsq, casts[tc])
            if tc > 0:
                emit_k(tc - 1)
            st = emit_ln_p1b(st_a)
            if tc > 0:
                emit_v(tc - 1)
                st = (st[0], st[1], st[2], cP.v)
            plan.wait("DVE", qS, z_store[tc])
            zdones[tc] = emit_ln_p2(ACH, st, xc, zc, tmp=t1)
            plan.wait("SP", cD, zdones[tc])
            z_store[tc + 1] = SPo(lambda e, sl=sl, b=b: e.dma_start(
                zd.ap()[b, :, :, sl].rearrange("t p f -> p t f"),
                zc[:, :].rearrange("p (t f) -> p t f", t=DT)), inc=(qS, 16))
            if tc + 1 < NACH:
                a1_load(tc + 1)
                a1_casts(tc + 1)
        emit_k(NACH - 1)
        emit_v(NACH - 1)

        plan.wait("SP", cP, cP.v)
        plan.wait("SP", cD, cD.v)
        plan.wait("SP", qS, qS.v)
        z_load = SPo(lambda e, b=b: e.dma_start(
            zc[:, :].rearrange("p (t f) -> p t f", t=DT),
            zd.ap()[b, :, :, 0:ACH].rearrange("t p f -> p t f")),
            inc=(qX, 16))
        z3_store_prev = 0
        x2_store_prev = 0
        xcr = None

        def emit_qproj(qcn, zl):
            plan.wait("PE", qX, zl)
            plan.wait("ACT", qS, z3_store_prev)
            return emit_proj(wq_s, zc, ACH, lambda mt: qc_s[:, ts(mt, ACH)],
                             bq_s, AF.Relu, w_rdy["wq"], None)

        q_ready = emit_qproj(0, z_load)
        plan.wait("SP", cP, cP.v)
        z_load = SPo(lambda e, b=b: e.dma_start(
            zc[:, :].rearrange("p (t f) -> p t f", t=DT),
            zd.ap()[b, :, :, ACH:2 * ACH].rearrange("t p f -> p t f")),
            inc=(qX, 16))

        for qcn in range(NACH):
            sl = slice(qcn * ACH, (qcn + 1) * ACH)
            nkc = qcn + 1
            nkt = 4 * nkc
            kw = nkc * 512
            p_avs = [None] * 4

            def emit_scores(r):
                sr = (srow, srow2)[r % 2]
                plan.wait("PE", cA, q_ready)
                for kc in range(nkc):
                    bk = next_bank()
                    bk.acquire(plan)
                    diag = (kc == nkc - 1)
                    pv = None
                    for dt in range(DT):
                        last_mm = (dt == DT - 1) and not diag
                        pv = PEo(lambda e, r=r, kc=kc, dt=dt, bk=bk,
                                 last_mm=last_mm: e.matmul(
                            bk.ap,
                            qc_s[:, dt * ACH + r * P: dt * ACH + (r + 1) * P],
                            kT[:, dt * SEQ + kc * 512:
                               dt * SEQ + (kc + 1) * 512],
                            start=(dt == 0), stop=last_mm),
                            inc=(cP, 1) if last_mm else None)
                    if diag:
                        pv = PEo(lambda e, r=r, bk=bk: e.matmul(
                            bk.ap, iden_s[:, :], mask_s[:, ts(r, 512)],
                            start=False, stop=True), inc=(cP, 1))
                    plan.wait("ACT", cP, pv)
                    if r % 2 == 1:
                        plan.wait("ACT", qS, z3_store_prev)
                    av = ACTo(lambda e, kc=kc, bk=bk, sr=sr: e.activation(
                        sr[:, ts(kc, 512)], bk.ap, AF.Copy), inc=(cA, 1))
                    bk.free = (cA, av)
                plan.wait("DVE", cA, cA.v)
                plan.wait("DVE", cD, cD.v)
                dv = DVEo(lambda e, kw=kw, sr=sr: e.reduce_max(
                    nmx[:, 0:1], sr[:, 0:kw], AX.X, negate=True), inc=(cD, 1))
                plan.wait("DVE", cD, dv)
                dv = DVEo(lambda e: e.tensor_scalar(
                    nmx2[:, 0:1], nmx[:, 0:1], ISD, None, ALU.mult),
                    inc=(cD, 1))
                plan.wait("ACT", cD, dv)
                av = ACTo(lambda e, kw=kw, sr=sr: e.activation(
                    xbp[:, 0:kw], sr[:, 0:kw], AF.Exp, bias=nmx2[:, 0:1],
                    scale=ISD, accum_out=esum[:, 0:1]), inc=(cA, 1))
                plan.wait("DVE", cA, av)
                dv = DVEo(lambda e: e.reciprocal(rcol[:, 0:1], esum[:, 0:1]),
                          inc=(cD, 1))
                plan.wait("ACT", cD, dv)
                po = (r % 2) * 2048
                p_avs[r] = ACTo(lambda e, kw=kw, po=po: e.activation(
                    xsq[:, po:po + kw], xbp[:, 0:kw], AF.Copy,
                    scale=rcol[:, 0:1]), inc=(cA, 1))

            def emit_transp(r):
                po = (r % 2) * 2048
                plan.wait("PE", cA, p_avs[r])
                for w0 in range(0, nkt, 8):
                    tp = (pAl, pAh)[(w0 // 8) % 2]
                    tp.acquire(plan)
                    n8 = min(8, nkt - w0)
                    for i in range(n8):
                        pv = PEo(lambda e, w0=w0, i=i, tp=tp, po=po:
                                 e.transpose(
                            tp.bf[:, ts(i, P)],
                            xsq[:, po + (w0 + i) * P: po + (w0 + i + 1) * P],
                            iden_s[:, :]),
                            inc=(cP, 1) if i == n8 - 1 else None)
                    plan.wait("DVE", cP, pv)
                    dv = None
                    for i in range(n8):
                        dv = DVEo(lambda e, w0=w0, i=i, r=r, tp=tp:
                                  e.tensor_copy(
                            ptb[:, (w0 + i) * ACH + r * P:
                                (w0 + i) * ACH + (r + 1) * P],
                            tp.bf[:, ts(i, P)]),
                            inc=(cD, 1) if i == n8 - 1 else None)
                    tp.free = (cD, dv)

            emit_scores(0)
            for r in range(1, 4):
                emit_scores(r)
                emit_transp(r - 1)
            emit_transp(3)
            pt_ready = cD.v

            plan.wait("SP", cA, cA.v)
            plan.wait("SP", cD, cD.v)
            plan.wait("SP", qS, qS.v)
            if xcr is not None:
                plan.wait("SP", cA, xcr[0])
                plan.wait("SP", cD, xcr[1])
            xl = SPo(lambda e, sl=sl, b=b: e.dma_start(
                xc[:, :].rearrange("p (t f) -> p t f", t=DT),
                xTd.ap()[b, :, :, sl].rearrange("t p f -> p t f")),
                inc=(qX, 16))

            plan.wait("PE", cD, pt_ready)
            y_ready = None
            for dm in range(DT):
                bk = next_bank()
                bk.acquire(plan)
                for kt in range(nkt):
                    pv = PEo(lambda e, dm=dm, kt=kt, bk=bk, nkt=nkt: e.matmul(
                        bk.ap, vtok[:, kt * D + dm * P: kt * D + (dm + 1) * P],
                        ptb[:, ts(kt, ACH)], start=(kt == 0),
                        stop=(kt == nkt - 1)),
                        inc=(cP, 1) if kt == nkt - 1 else None)
                plan.wait("ACT", cP, pv)
                y_ready = ACTo(lambda e, dm=dm, bk=bk: e.activation(
                    qc_s[:, ts(dm, ACH)], bk.ap, AF.Copy), inc=(cA, 1))
                bk.free = (cA, y_ready)
            plan.wait("PE", cA, y_ready)
            ao_ready = emit_proj(wo_s, qc_s, ACH,
                                 lambda mt: xbp[:, ts(mt, ACH)],
                                 bo_s, AF.Relu, w_rdy["wo"], None)
            if qcn == NACH - 1:
                f_gate = [(cP, cP.v), (cA, cA.v), (cD, cD.v)]
            plan.wait("ACT", cA, ao_ready)
            plan.wait("ACT", cP, cP.v)
            cast_av = ACTo(lambda e: e.activation(xsq[:, :], xbp[:, :],
                                                  AF.Square), inc=(cA, 1))
            st2 = emit_ln_p1(ACH, xbp, xsq, cast_av)
            if qcn + 1 < NACH:
                q_ready = emit_qproj(qcn + 1, z_load)
                if qcn + 2 < NACH:
                    plan.wait("SP", cP, cP.v)
                    nsl = slice((qcn + 2) * ACH, (qcn + 3) * ACH)
                    z_load = SPo(lambda e, nsl=nsl, b=b: e.dma_start(
                        zc[:, :].rearrange("p (t f) -> p t f", t=DT),
                        zd.ap()[b, :, :, nsl].rearrange("t p f -> p t f")),
                        inc=(qX, 16))
            plan.wait("DVE", qX, xl)

            def apply_res(slz, Rap, Aap, last):
                DVEo(lambda e, slz=slz, Rap=Rap: e.tensor_mul(
                    t1[:, 0:512], xbp[:, slz], Rap))
                DVEo(lambda e, Aap=Aap: e.tensor_add(
                    t2[:, 0:512], t1[:, 0:512], Aap))
                return DVEo(lambda e, slz=slz: e.tensor_add(
                    xc[:, slz], xc[:, slz], t2[:, 0:512]),
                    inc=(cD, 1) if last else None)

            res_dv = emit_ln_p2(ACH, st2, xbp, None, apply_cb=apply_res)
            plan.wait("SP", cD, res_dv)
            x2_store_prev = SPo(lambda e, sl=sl, b=b: e.dma_start(
                x2d.ap()[b, :, :, sl].rearrange("t p f -> p t f"),
                xc[:, :].rearrange("p (t f) -> p t f", t=DT)), inc=(qS, 16))

            plan.wait("ACT", cD, res_dv)
            ACTo(lambda e: e.activation(xbp[:, :], xc[:, :], AF.Copy))
            cast_av = ACTo(lambda e: e.activation(xsq[:, :], xc[:, :],
                                                  AF.Square), inc=(cA, 1))
            st3 = emit_ln_p1(ACH, xbp, xsq, cast_av)
            plan.wait("DVE", qS, x2_store_prev)
            z3_dv = emit_ln_p2(ACH, st3, xc, z3c, tmp=t1)
            xcr = (cast_av, z3_dv)
            plan.wait("SP", cD, z3_dv)
            z3_store_prev = SPo(lambda e, sl=sl, b=b: e.dma_start(
                z3d.ap()[b, :, :, sl].rearrange("t p f -> p t f"),
                z3c[:, :].rearrange("p (t f) -> p t f", t=DT)), inc=(qS, 16))
            if qcn == 1:
                z3st_early = z3_store_prev

        a2_end_cd = cD.v
        a2_end_qs = qS.v

        for s2, v2 in f_gate:
            plan.wait("PL", s2, v2)

        def stream_slab(dsth, src_d, mt):
            if id(dsth) in dst_prev_users:
                plan.wait("PL", *dst_prev_users[id(dsth)])
            return PLo(lambda g, d=dsth, s=src_d, mt=mt: g.dma_start(
                d[:, :], s.ap()[mt]), inc=(qW, 16))

        z3l = None
        for fc in range(NFCH):
            fsl = slice(fc * FCH, (fc + 1) * FCH)
            if fc == 0:
                for s2, v2 in f_gate:
                    plan.wait("SP", s2, v2)
                plan.wait("SP", cD, a2_end_cd)
                plan.wait("SP", qS, z3st_early)
                z3l = SPo(lambda e, fsl=fsl, b=b: e.dma_start(
                    z3[:, :].rearrange("p (t f) -> p t f", t=DT),
                    z3d.ap()[b, :, :, fsl].rearrange("t p f -> p t f")),
                    inc=(qX, 16))

            plan.wait("PE", qX, z3l)
            a_last = None
            for mt in range(HT):
                k1, k2 = wsl[2 * (mt % 2)], wsl[2 * (mt % 2) + 1]
                r1 = stream_slab(k1, w1_d, mt)
                r2 = stream_slab(k2, w2_d, mt)
                plan.wait("PE", qW, r2)
                pvU = pvV = None
                for j in range(2):
                    bkU = next_bank()
                    bkU.acquire(plan)
                    for kt in range(DT):
                        pvU = PEo(lambda e, j=j, kt=kt, bk=bkU, w=k1: e.matmul(
                            bk.ap, w[:, ts(kt, P)],
                            z3[:, kt * FCH + j * 512: kt * FCH + (j + 1) * 512],
                            start=(kt == 0), stop=(kt == DT - 1)))
                    bkV = next_bank()
                    bkV.acquire(plan)
                    for kt in range(DT):
                        pvV = PEo(lambda e, j=j, kt=kt, bk=bkV, w=k2: e.matmul(
                            bk.ap, w[:, ts(kt, P)],
                            z3[:, kt * FCH + j * 512: kt * FCH + (j + 1) * 512],
                            start=(kt == 0), stop=(kt == DT - 1)),
                            inc=(cP, 1) if kt == DT - 1 else None)
                    uu = u_s[uflip[0] % 4]
                    vv = v2_s[uflip[0] % 4]
                    uflip[0] += 1
                    pvU = pvV
                    plan.wait("ACT", cP, pvV)
                    if id(uu) in dst_prev_users:
                        plan.wait("ACT", *dst_prev_users[id(uu)])
                    av = ACTo(lambda e, mt=mt, bk=bkU, uu=uu: e.activation(
                        uu[:, :], bk.ap, AF.Silu, scale=s1_s[:, mt:mt + 1]),
                        inc=(cA, 1))
                    bkU.free = (cA, av)
                    av = ACTo(lambda e, mt=mt, bk=bkV, vv=vv: e.activation(
                        vv[:, :], bk.ap, AF.Copy, scale=s2_s[:, mt:mt + 1]),
                        inc=(cA, 1))
                    bkV.free = (cA, av)
                    plan.wait("DVE", cA, av)
                    if mt == 0 and j == 0:
                        plan.wait("DVE", qS, qS.v)
                    a_last = DVEo(lambda e, mt=mt, j=j, uu=uu, vv=vv:
                                  e.tensor_mul(
                        abuf[:, mt * FCH + j * 512: mt * FCH + (j + 1) * 512],
                        uu[:, :], vv[:, :]), inc=(cD, 1))
                    dst_prev_users[id(uu)] = (cD, a_last)
                dst_prev_users[id(k1)] = (cP, pvU)
                dst_prev_users[id(k2)] = (cP, pvV)
            a_ready = a_last

            def h_stage(src_h, dst_h, h_dd, hs_col, rdy_sem, rdy_val):
                plan.wait("PE", rdy_sem, rdy_val)
                last_av = None
                pv = None
                for mt in range(HT):
                    slab = hsl[mt % 2]
                    rr = stream_slab(slab, h_dd, mt)
                    plan.wait("PE", qW, rr)
                    bks = []
                    for j in range(2):
                        bk = next_bank()
                        bk.acquire(plan)
                        bks.append(bk)
                        for kt in range(HT):
                            pv = PEo(lambda e, j=j, kt=kt, bk=bk, sh=src_h,
                                     slab=slab: e.matmul(
                                bk.ap, slab[:, ts(kt, P)],
                                sh[:, kt * FCH + j * 512:
                                   kt * FCH + (j + 1) * 512],
                                start=(kt == 0), stop=(kt == HT - 1)),
                                inc=(cP, 1)
                                if (kt == HT - 1 and j == 1) else None)
                    plan.wait("ACT", cP, pv)
                    for j in range(2):
                        last_av = ACTo(lambda e, mt=mt, j=j, bk=bks[j],
                                       dh=dst_h, hc=hs_col: e.activation(
                            dh[:, mt * FCH + j * 512: mt * FCH + (j + 1) * 512],
                            bk.ap, AF.Gelu, scale=hc[:, mt:mt + 1]),
                            inc=(cA, 1))
                        bks[j].free = (cA, last_av)
                    dst_prev_users[id(slab)] = (cP, pv)
                return last_av

            b_ready = h_stage(abuf, bbuf, h1_d, h1s_s, cD, a_ready)
            c_ready = h_stage(bbuf, abuf, h2_d, h2s_s, cA, b_ready)

            plan.wait("PE", cA, c_ready)
            fdone = None
            w3_pv = None
            for mt in range(DT):
                slab = hsl[mt % 2]
                rr = stream_slab(slab, w3_d, mt)
                plan.wait("PE", qW, rr)
                for j in range(2):
                    bk = next_bank()
                    bk.acquire(plan)
                    for kt in range(HT):
                        w3_pv = PEo(lambda e, j=j, kt=kt, bk=bk, slab=slab:
                                    e.matmul(
                            bk.ap, slab[:, ts(kt, P)],
                            abuf[:, kt * FCH + j * 512:
                                 kt * FCH + (j + 1) * 512],
                            start=(kt == 0), stop=(kt == HT - 1)),
                            inc=(cP, 1) if kt == HT - 1 else None)
                    plan.wait("ACT", cP, w3_pv)
                    fdone = ACTo(lambda e, mt=mt, j=j, bk=bk: e.activation(
                        fbuf[:, mt * FCH + j * 512: mt * FCH + (j + 1) * 512],
                        bk.ap, AF.Copy, scale=w3s_s[:, mt:mt + 1]),
                        inc=(cA, 1))
                    bk.free = (cA, fdone)
                dst_prev_users[id(slab)] = (cP, w3_pv)

            if fc + 1 < NFCH:
                plan.wait("SP", cP, cP.v)
                plan.wait("SP", qS, a2_end_qs)
                nfsl = slice((fc + 1) * FCH, (fc + 2) * FCH)
                z3l = SPo(lambda e, nfsl=nfsl, b=b: e.dma_start(
                    z3[:, :].rearrange("p (t f) -> p t f", t=DT),
                    z3d.ap()[b, :, :, nfsl].rearrange("t p f -> p t f")),
                    inc=(qX, 16))

            plan.wait("SP", cP, w3_pv)
            x2l2 = SPo(lambda e, fsl=fsl, b=b: e.dma_start(
                x2c2[:, :].rearrange("p (t f) -> p t f", t=DT),
                x2d.ap()[b, :, :, fsl].rearrange("t p f -> p t f")),
                inc=(qX, 16))

            plan.wait("ACT", cA, fdone)
            plan.wait("ACT", cD, a2_end_cd)
            plan.wait("ACT", qS, a2_end_qs)
            ACTo(lambda e: e.activation(fb16[:, :], fbuf[:, :], AF.Copy))
            cast_av = ACTo(lambda e: e.activation(fsq16[:, :], fbuf[:, :],
                                                  AF.Square), inc=(cA, 1))
            st4 = emit_ln_p1(FCH, fb16, fsq16, cast_av)
            if fc == NFCH - 1:
                prevF_ln4 = (cast_av, cP.v)
            plan.wait("DVE", qX, x2l2)

            def apply_res4(slz, Rap, Aap, last):
                DVEo(lambda e, slz=slz, Rap=Rap: e.tensor_mul(
                    t3[:, :], fbuf[:, slz], Rap))
                DVEo(lambda e, Aap=Aap: e.tensor_add(
                    t4[:, :], t3[:, :], Aap))
                return DVEo(lambda e, slz=slz: e.tensor_add(
                    x2c2[:, slz], x2c2[:, slz], t4[:, :]),
                    inc=(cD, 1) if last else None)

            res_dv = emit_ln_p2(FCH, st4, fbuf, None, apply_cb=apply_res4)
            plan.wait("SP", cD, res_dv)
            SPo(lambda e, fsl=fsl, b=b: e.dma_start(
                outd.ap()[b, :, :, fsl].rearrange("t p f -> p t f"),
                x2c2[:, :].rearrange("p (t f) -> p t f", t=DT)), inc=(qS, 16))

        attn_gate = [(cP, w3_pv)]
        prevF_cd = cD.v
        prevF_qs = qS.v

    plan.wait("SP", qS, qS.v)

    with nc.Block() as block:
        @block.tensor
        def _(eng):
            for f in plan.E["PE"]:
                f(eng)

        @block.scalar
        def _(eng):
            for f in plan.E["ACT"]:
                f(eng)

        @block.vector
        def _(eng):
            for f in plan.E["DVE"]:
                f(eng)

        @block.sync
        def _(eng):
            for f in plan.E["SP"]:
                f(eng)

        @block.gpsimd
        def _(eng):
            for f in plan.E["PL"]:
                f(eng)

    return nc



def _prep_weights(inp):
    import ml_dtypes as mld
    f32 = np.float32

    def lhsT_pack(w_din_dout, dt_in, dt_out):
        w = np.ascontiguousarray(np.asarray(w_din_dout, f32))
        t = w.reshape(dt_in, P, dt_out, P).transpose(1, 2, 0, 3)
        return np.ascontiguousarray(t.reshape(P, dt_out * dt_in * P))

    def slab_pack(w_din_dout, dt_in, dt_out):
        w = np.ascontiguousarray(np.asarray(w_din_dout, f32))
        t = w.reshape(dt_in, P, dt_out, P).transpose(2, 1, 0, 3)
        return np.ascontiguousarray(t.reshape(dt_out, P, dt_in * P))

    def col(v, nt):
        return np.ascontiguousarray(np.asarray(v, f32).reshape(nt, P).T)

    mask = np.zeros((P, NACH * ACH), dtype=f32)
    i = np.arange(P)[:, None]
    j = np.arange(ACH)[None, :]
    for v in range(NACH):
        mask[:, v * ACH:(v + 1) * ACH] = np.where(j > i + P * v, MASKVAL, 0.0)

    return {
        "wq": lhsT_pack(inp["L0_attn_Wq"], DT, DT),
        "wk": lhsT_pack(inp["L0_attn_Wk"], DT, DT),
        "wo": lhsT_pack(inp["L0_attn_Wo"], DT, DT),
        "wv": np.ascontiguousarray(
            np.asarray(inp["L0_attn_Wv"], f32).reshape(DT, P, D)
            .transpose(1, 0, 2).reshape(P, DT * D)),
        "w1p": slab_pack(np.asarray(inp["L0_ffn_w1_w"], f32).T, DT, HT),
        "w2p": slab_pack(np.asarray(inp["L0_ffn_w2_w"], f32).T, DT, HT),
        "h1p": slab_pack(np.asarray(inp["L0_ffn_h1_w"], f32).T, HT, HT),
        "h2p": slab_pack(np.asarray(inp["L0_ffn_h2_w"], f32).T, HT, HT),
        "w3p": slab_pack(np.asarray(inp["L0_ffn_w3_w"], f32).T, HT, DT),
        "bq": col(inp["L0_attn_bq"], DT),
        "bk": col(inp["L0_attn_bk"], DT),
        "bo": col(inp["L0_attn_bo"], DT),
        "bvb": np.ascontiguousarray(
            np.asarray(inp["L0_attn_bv"], f32).reshape(1, D)),
        "s1c": col(inp["L0_ffn_w1_s"], HT),
        "s2c": col(inp["L0_ffn_w2_s"], HT),
        "h1sc": col(inp["L0_ffn_h1_s"], HT),
        "h2sc": col(inp["L0_ffn_h2_s"], HT),
        "w3sc": col(inp["L0_ffn_w3_s"], DT),
        "mask": mask,
        "iden": np.eye(P, dtype=mld.bfloat16),
        "onesc": np.ones((P, 1), dtype=mld.bfloat16),
        "onesr": np.ones((1, P), dtype=np.float32),
    }


def kernel(**inputs):
    x = np.asarray(inputs["L0_attn_x"], dtype=np.float32)
    shared = _prep_weights(inputs)
    nc = build_program()
    in_maps = []
    for c in range(NCORES):
        m = dict(shared)
        m["xT"] = np.ascontiguousarray(
            x[c * NB:(c + 1) * NB].transpose(0, 2, 1)).reshape(NB, DT, P, SEQ)
        in_maps.append(m)
    res = run_bass_kernel_spmd(nc, in_maps, core_ids=list(range(NCORES)))
    out = np.empty((NCORES * NB, SEQ, D), dtype=np.float32)
    for c in range(NCORES):
        y = np.asarray(res.results[c]["outT"])
        out[c * NB:(c + 1) * NB] = y.reshape(NB, D, SEQ).transpose(0, 2, 1)
    return out


# version 13
# speedup vs baseline: 1.0251x; 1.0251x over previous
import numpy as np
from contextlib import ExitStack

import concourse.bass as bass
import concourse.mybir as mybir
from concourse.bass_utils import run_bass_kernel_spmd

F32 = mybir.dt.float32
BF16 = mybir.dt.bfloat16
AF = mybir.ActivationFunctionType
ALU = mybir.AluOpType
AX = mybir.AxisListType

P = 128
D = 1024
DT = D // P
H = 4096
HT = H // P
SEQ = 2048
NB = 4
NCORES = 8
ACH = 512
FCH = 1024
NACH = SEQ // ACH
NFCH = SEQ // FCH
MASKVAL = -320000.0
ISD = 1.0 / 32.0
EPS = 1e-5
KiB = 1024


def ts(i, n):
    return slice(i * n, (i + 1) * n)


class Sem:
    def __init__(self, handle, name):
        self.h = handle
        self.name = name
        self.v = 0


class Plan:
    ENGINES = ("PE", "ACT", "DVE", "SP", "PL")

    def __init__(self):
        self.E = {k: [] for k in self.ENGINES}
        self.waited = {k: {} for k in self.ENGINES}

    def wait(self, en, sem, val):
        if val is None or val <= 0:
            return
        if self.waited[en].get(sem.name, 0) >= val:
            return
        self.waited[en][sem.name] = val
        self.E[en].append(lambda eng, s=sem.h, v=val: eng.wait_ge(s, v))

    def op(self, en, fn, inc=None):
        if inc is not None:
            sem, n = inc
            sem.v += n
            tgt = sem.v
            self.E[en].append(
                lambda eng, fn=fn, s=sem.h, n=n: fn(eng).then_inc(s, n))
            return tgt
        self.E[en].append(lambda eng, fn=fn: fn(eng))
        return None


class PBank:
    def __init__(self, ap, bf=None):
        self.ap = ap
        self.bf = bf
        self.free = None

    def acquire(self, plan, en="PE"):
        if self.free is not None:
            plan.wait(en, *self.free)


def build_program():
    nc = bass.Bass('TRN2', target_bir_lowering=False)

    xTd = nc.dram_tensor("xT", [NB, DT, P, SEQ], F32, kind="ExternalInput")
    outd = nc.dram_tensor("outT", [NB, DT, P, SEQ], F32, kind="ExternalOutput")
    x2d = nc.dram_tensor("x2scr", [NB, DT, P, SEQ], F32, kind="Internal")
    zd = nc.dram_tensor("zscr", [NB, DT, P, SEQ], BF16, kind="Internal")
    z3d = nc.dram_tensor("z3scr", [NB, DT, P, SEQ], BF16, kind="Internal")

    wq_d = nc.dram_tensor("wq", [P, DT * DT * P], F32, kind="ExternalInput")
    wk_d = nc.dram_tensor("wk", [P, DT * DT * P], F32, kind="ExternalInput")
    wo_d = nc.dram_tensor("wo", [P, DT * DT * P], F32, kind="ExternalInput")
    wv_d = nc.dram_tensor("wv", [P, DT * D], F32, kind="ExternalInput")
    w1_d = nc.dram_tensor("w1p", [HT, P, DT * P], F32, kind="ExternalInput")
    w2_d = nc.dram_tensor("w2p", [HT, P, DT * P], F32, kind="ExternalInput")
    h1_d = nc.dram_tensor("h1p", [HT, P, HT * P], F32, kind="ExternalInput")
    h2_d = nc.dram_tensor("h2p", [HT, P, HT * P], F32, kind="ExternalInput")
    w3_d = nc.dram_tensor("w3p", [DT, P, HT * P], F32, kind="ExternalInput")
    bq_d = nc.dram_tensor("bq", [P, DT], F32, kind="ExternalInput")
    bk_d = nc.dram_tensor("bk", [P, DT], F32, kind="ExternalInput")
    bo_d = nc.dram_tensor("bo", [P, DT], F32, kind="ExternalInput")
    bvb_d = nc.dram_tensor("bvb", [1, D], F32, kind="ExternalInput")
    s1_d = nc.dram_tensor("s1c", [P, HT], F32, kind="ExternalInput")
    s2_d = nc.dram_tensor("s2c", [P, HT], F32, kind="ExternalInput")
    h1s_d = nc.dram_tensor("h1sc", [P, HT], F32, kind="ExternalInput")
    h2s_d = nc.dram_tensor("h2sc", [P, HT], F32, kind="ExternalInput")
    w3s_d = nc.dram_tensor("w3sc", [P, DT], F32, kind="ExternalInput")
    mask_d = nc.dram_tensor("mask", [P, NACH * ACH], F32, kind="ExternalInput")
    iden_d = nc.dram_tensor("iden", [P, P], BF16, kind="ExternalInput")
    onesc_d = nc.dram_tensor("onesc", [P, 1], BF16, kind="ExternalInput")
    onesr_d = nc.dram_tensor("onesr", [1, P], F32, kind="ExternalInput")

    plan = Plan()
    PEo = lambda fn, inc=None: plan.op("PE", fn, inc)
    ACTo = lambda fn, inc=None: plan.op("ACT", fn, inc)
    DVEo = lambda fn, inc=None: plan.op("DVE", fn, inc)
    SPo = lambda fn, inc=None: plan.op("SP", fn, inc)
    PLo = lambda fn, inc=None: plan.op("PL", fn, inc)

    mksem = lambda n: Sem(nc.alloc_semaphore(name=n), n)
    qW = mksem("qW")
    qX = mksem("qX")
    qS = mksem("qS")
    cP = mksem("cP")
    cA = mksem("cA")
    cD = mksem("cD")

    sb = lambda name, shape, dtype: nc.alloc_sbuf_tensor(name, shape, dtype)
    iden_s = sb("iden_s", [P, P], BF16)
    onesc_s = sb("onesc_s", [P, 1], BF16)
    onesr_s = sb("onesr_s", [1, P], F32)
    eps_s = sb("eps_s", [1, 1], F32)
    bq_s = sb("bq_s", [P, DT], F32)
    bk_s = sb("bk_s", [P, DT], F32)
    bo_s = sb("bo_s", [P, DT], F32)
    bvb_s = sb("bvb_s", [1, D], F32)
    s1_s = sb("s1_s", [P, HT], F32)
    s2_s = sb("s2_s", [P, HT], F32)
    h1s_s = sb("h1s_s", [P, HT], F32)
    h2s_s = sb("h2s_s", [P, HT], F32)
    w3s_s = sb("w3s_s", [P, DT], F32)
    mask_s = sb("mask_s", [P, NACH * ACH], BF16)
    mrow = sb("mrow", [1, FCH], F32)
    vrow = sb("vrow", [1, FCH], F32)
    rrow = sb("rrow", [1, FCH], F32)
    arow = vrow
    nmx = sb("nmx", [P, 1], F32)
    nmx2 = sb("nmx2", [P, 1], F32)
    esum = sb("esum", [P, 1], F32)
    rcol = sb("rcol", [P, 1], F32)
    marker = nc.alloc_sbuf_tensor("arena_marker", [P, 32], mybir.dt.uint8)
    base = nc.lookup_mloc(marker).addr

    def at(name, shape, dtype, off):
        return nc.alloc_sbuf_tensor_at(name, shape, dtype, offset=base + off)

    wq_s = at("wq_s", [P, DT * DT * P], BF16, 0)
    wo_s = at("wo_s", [P, DT * DT * P], BF16, 16 * KiB)
    kT = at("kT", [P, DT * SEQ], BF16, 32 * KiB)
    vtok = at("vtok", [P, (SEQ // P) * D], BF16, 64 * KiB)
    xc = at("xc", [P, DT * ACH], F32, 96 * KiB)
    xbp = at("xbp", [P, DT * ACH], BF16, 112 * KiB)
    xsq = at("xsq", [P, DT * ACH], BF16, 120 * KiB)
    zc = at("zc", [P, DT * ACH], BF16, 128 * KiB)
    t1 = at("t1", [P, 512], F32, 136 * KiB)
    t2 = at("t2", [P, 512], F32, 138 * KiB)
    wk_s = at("wk_s", [P, DT * DT * P], BF16, 140 * KiB)
    wv_s = at("wv_s", [P, DT * D], BF16, 156 * KiB)
    qc_s = at("qc_s", [P, DT * ACH], BF16, 140 * KiB)
    srow = at("srow", [P, SEQ], F32, 148 * KiB)
    ptb = at("ptb", [P, (SEQ // P) * ACH], BF16, 156 * KiB)
    srow2 = at("srow2", [P, SEQ], F32, 96 * KiB)
    z3c = at("z3c", [P, DT * ACH], BF16, 96 * KiB)
    A_END = 172 * KiB

    abuf = at("abuf", [P, HT * FCH], BF16, 0)
    x2c2 = at("x2c2", [P, DT * FCH], F32, 0)
    fb16 = at("fb16", [P, DT * FCH], BF16, 96 * KiB)
    fsq16 = at("fsq16", [P, DT * FCH], BF16, 112 * KiB)
    bbuf = at("bbuf", [P, HT * FCH], BF16, 64 * KiB)
    fbuf = at("fbuf", [P, DT * FCH], F32, 64 * KiB)
    z3 = at("z3", [P, DT * FCH], BF16, 128 * KiB)
    wsl = [at(f"wsl{i}", [P, DT * P], BF16, 144 * KiB + i * 2 * KiB)
           for i in range(4)]
    hsl = [at(f"hsl{i}", [P, HT * P], BF16, 152 * KiB + i * 8 * KiB)
           for i in range(2)]
    t3 = at("t3", [P, 512], F32, 180 * KiB)
    t4 = at("t4", [P, 512], F32, 182 * KiB)
    u_s = [at(f"u{i}", [P, 512], BF16, 172 * KiB + i * KiB)
           for i in range(4)]
    v2_s = [at(f"v2{i}", [P, 512], BF16, 176 * KiB + i * KiB)
            for i in range(4)]
    F_END = 184 * KiB
    assert base + max(A_END, F_END) <= nc.sbuf_top, (
        base, A_END, F_END, nc.sbuf_top)

    pb_t = [nc.alloc_psum_tensor(f"pb{i}", [P, 512], F32) for i in range(4)]
    pR_t = nc.alloc_psum_tensor("pR", [P, 1024], F32)
    pA_t = nc.alloc_psum_tensor("pA", [P, 1024], F32)
    pbanks = [PBank(h[:, :]) for h in pb_t]
    pA_bf = pA_t.ap().bitcast(BF16)
    pRl = PBank(pR_t[:, 0:512])
    pRh = PBank(pR_t[:, 512:1024])
    pAl = PBank(pA_t[:, 0:512], pA_bf[:, 0:1024])
    pAh = PBank(pA_t[:, 512:1024], pA_bf[:, 1024:2048])
    rot = [0]

    def next_bank():
        bk = pbanks[rot[0] % 4]
        rot[0] += 1
        return bk

    PLo(lambda g: g.memset(eps_s[0:1, 0:1], EPS), inc=(qW, 1))
    for dst, src in ((iden_s, iden_d), (onesc_s, onesc_d), (onesr_s, onesr_d),
                     (bq_s, bq_d), (bk_s, bk_d), (bo_s, bo_d), (bvb_s, bvb_d),
                     (s1_s, s1_d), (s2_s, s2_d), (h1s_s, h1s_d),
                     (h2s_s, h2s_d), (w3s_s, w3s_d), (mask_s, mask_d)):
        PLo(lambda g, d=dst, s=src: g.dma_start(d[:, :], s.ap()), inc=(qW, 16))
    for en in ("PE", "ACT", "DVE", "SP"):
        plan.wait(en, qW, qW.v)

    def emit_ln_p1a(width, xb_h, xq_h, ready_act):
        nw = width // 512
        plan.wait("PE", cA, ready_act)
        stat = [next_bank() for _ in range(2 * nw)]
        pv = None
        for j in range(nw):
            for idx, sh in ((0, xb_h), (1, xq_h)):
                bk = stat[2 * j + idx]
                bk.acquire(plan)
                for t in range(DT):
                    pv = PEo(lambda e, j=j, t=t, bk=bk, sh=sh: e.matmul(
                        bk.ap[0:1, :], onesc_s[:, 0:1],
                        sh[:, t * width + j * 512: t * width + (j + 1) * 512],
                        start=(t == 0), stop=(t == DT - 1)),
                        inc=(cP, 1) if t == DT - 1 else None)
        plan.wait("DVE", cP, pv)
        dv = None
        for j in range(nw):
            DVEo(lambda e, j=j, bk=stat[2 * j]: e.tensor_scalar(
                mrow[0:1, ts(j, 512)], bk.ap[0:1, :], 1.0 / D, None, ALU.mult))
            DVEo(lambda e, j=j: e.tensor_mul(
                vrow[0:1, ts(j, 512)], mrow[0:1, ts(j, 512)],
                mrow[0:1, ts(j, 512)]))
            dv = DVEo(lambda e, j=j, bk=stat[2 * j + 1]: e.scalar_tensor_tensor(
                vrow[0:1, ts(j, 512)], bk.ap[0:1, :], 1.0 / D,
                vrow[0:1, ts(j, 512)], ALU.mult, ALU.subtract), inc=(cD, 1))
        for s in stat:
            s.free = (cD, dv)
        plan.wait("ACT", cD, dv)
        ACTo(lambda e: e.activation(rrow[0:1, 0:width], vrow[0:1, 0:width],
                                    AF.Ln, bias=eps_s[0:1, 0:1]))
        av = ACTo(lambda e: e.activation(rrow[0:1, 0:width], rrow[0:1, 0:width],
                                         AF.Exp, scale=-0.5), inc=(cA, 1))
        plan.wait("DVE", cA, av)
        dv = DVEo(lambda e: e.scalar_tensor_tensor(
            arow[0:1, 0:width], mrow[0:1, 0:width], -1.0, rrow[0:1, 0:width],
            ALU.mult, ALU.mult), inc=(cD, 1))
        return (nw, dv)

    def emit_ln_p1b(st_a):
        nw, dv = st_a
        plan.wait("PE", cD, dv)
        Rb = [pRl, pRh][:nw]
        Ab = [pAl, pAh][:nw]
        pv = None
        for j in range(nw):
            Rb[j].acquire(plan)
            pv = PEo(lambda e, j=j, bk=Rb[j]: e.matmul(
                bk.ap, onesr_s[0:1, :], rrow[0:1, ts(j, 512)],
                start=True, stop=True), inc=(cP, 1))
        for j in range(nw):
            Ab[j].acquire(plan)
            pv = PEo(lambda e, j=j, bk=Ab[j]: e.matmul(
                bk.ap, onesr_s[0:1, :], arow[0:1, ts(j, 512)],
                start=True, stop=True), inc=(cP, 1))
        return (nw, Rb, Ab, pv)

    def emit_ln_p1(width, xb_h, xq_h, ready_act):
        return emit_ln_p1b(emit_ln_p1a(width, xb_h, xq_h, ready_act))

    def emit_ln_p2(width, st, src_h, dst_h, apply_cb=None, tmp=None):
        nw, Rb, Ab, pv = st
        plan.wait("DVE", cP, pv)
        dv = None
        for t in range(DT):
            for j in range(nw):
                slz = slice(t * width + j * 512, t * width + (j + 1) * 512)
                last = (t == DT - 1 and j == nw - 1)
                if apply_cb is None:
                    DVEo(lambda e, slz=slz, j=j, tmp=tmp: e.tensor_mul(
                        tmp[:, 0:512], src_h[:, slz], Rb[j].ap))
                    dv = DVEo(lambda e, slz=slz, j=j, tmp=tmp: e.tensor_add(
                        dst_h[:, slz], tmp[:, 0:512], Ab[j].ap),
                        inc=(cD, 1) if last else None)
                else:
                    dv = apply_cb(slz, Rb[j].ap, Ab[j].ap, last)
        for j in range(nw):
            Rb[j].free = (cD, dv)
            Ab[j].free = (cD, dv)
        return dv

    def emit_ln(width, xb_h, xq_h, src_h, dst_h, ready_act, apply_cb=None,
                tmp=None):
        st = emit_ln_p1(width, xb_h, xq_h, ready_act)
        return emit_ln_p2(width, st, src_h, dst_h, apply_cb, tmp)

    def emit_proj(w_h, z_h, width, dst_fn, bias_col, func, w_ready, z_rdy_cD):
        plan.wait("PE", cD, z_rdy_cD)
        plan.wait("PE", qW, w_ready)
        av = None
        for mt in range(DT):
            bk = next_bank()
            bk.acquire(plan)
            pv = None
            for kt in range(DT):
                pv = PEo(lambda e, mt=mt, kt=kt, bk=bk: e.matmul(
                    bk.ap, w_h[:, (mt * DT + kt) * P:(mt * DT + kt + 1) * P],
                    z_h[:, ts(kt, width)], start=(kt == 0),
                    stop=(kt == DT - 1)), inc=(cP, 1) if kt == DT - 1 else None)
            plan.wait("ACT", cP, pv)
            if bias_col is not None:
                av = ACTo(lambda e, mt=mt, bk=bk: e.activation(
                    dst_fn(mt), bk.ap, func, bias=bias_col[:, mt:mt + 1]),
                    inc=(cA, 1))
            else:
                av = ACTo(lambda e, mt=mt, bk=bk: e.activation(
                    dst_fn(mt), bk.ap, func), inc=(cA, 1))
            bk.free = (cA, av)
        return av

    prev_gate = []
    dst_prev_users = {}
    attn_gate = []
    prevF_cd = 0
    prevF_qs = 0
    prevF_ln4 = None
    uflip = [0]

    for b in range(NB):
        for s2, v2 in attn_gate:
            plan.wait("PL", s2, v2)
        w_rdy = {}
        for nm, dsth, srcd in (("wk", wk_s, wk_d), ("wv", wv_s, wv_d),
                               ("wq", wq_s, wq_d), ("wo", wo_s, wo_d)):
            if nm == "wq":
                plan.wait("PL", cD, prevF_cd)
                plan.wait("PL", qS, prevF_qs)
            if id(dsth) in dst_prev_users:
                plan.wait("PL", *dst_prev_users[id(dsth)])
            w_rdy[nm] = PLo(lambda g, d=dsth, s=srcd: g.dma_start(
                d[:, :], s.ap()), inc=(qW, 16))

        plan.wait("ACT", cD, prevF_cd)
        plan.wait("ACT", qS, prevF_qs)

        xls = [None] * NACH
        casts = [None] * NACH
        z_store = [0] * (NACH + 1)

        def a1_load(tc):
            if tc == 0 and prevF_ln4 is not None:
                plan.wait("SP", cP, prevF_ln4[1])
                plan.wait("SP", cA, prevF_ln4[0])
            else:
                plan.wait("SP", cP, cP.v)
                plan.wait("SP", cA, cA.v)
                plan.wait("SP", cD, cD.v)
            sl = slice(tc * ACH, (tc + 1) * ACH)
            xls[tc] = SPo(lambda e, sl=sl, b=b: e.dma_start(
                xc[:, :].rearrange("p (t f) -> p t f", t=DT),
                xTd.ap()[b, :, :, sl].rearrange("t p f -> p t f")),
                inc=(qX, 16))

        def a1_casts(tc):
            plan.wait("ACT", qX, xls[tc])
            if tc == 0 and prevF_ln4 is not None:
                plan.wait("ACT", cP, prevF_ln4[1])
            else:
                plan.wait("ACT", cP, cP.v)
            ACTo(lambda e: e.activation(xbp[:, :], xc[:, :], AF.Copy))
            casts[tc] = ACTo(lambda e: e.activation(
                xsq[:, :], xc[:, :], AF.Square), inc=(cA, 1))

        a1_load(0)
        a1_casts(0)
        last_v_av = None
        zdones = [None] * NACH
        vb_pending = [None]

        def emit_k(tc):
            emit_proj(wk_s, zc, ACH,
                      lambda mt, tc=tc: kT[:, mt * SEQ + tc * ACH:
                                           mt * SEQ + (tc + 1) * ACH],
                      bk_s, AF.Relu, w_rdy["wk"], zdones[tc])

        def emit_v(tc, groups):
            nonlocal last_v_av
            plan.wait("PE", qW, w_rdy["wv"])
            for tt, nch in groups:
                bk = next_bank()
                bk.acquire(plan)
                for kt in range(DT):
                    PEo(lambda e, tt=tt, nch=nch, kt=kt, bk=bk: e.matmul(
                        bk.ap,
                        zc[:, kt * ACH + tt * P: kt * ACH + (tt + 1) * P],
                        wv_s[:, kt * D + nch * 512:
                             kt * D + (nch + 1) * 512],
                        start=(kt == 0), stop=False))
                pv = PEo(lambda e, nch=nch, bk=bk: e.matmul(
                    bk.ap, onesr_s[0:1, :], bvb_s[0:1, ts(nch, 512)],
                    start=False, stop=True), inc=(cP, 1))
                plan.wait("ACT", cP, pv)
                gt = tc * (ACH // P) + tt
                last_v_av = ACTo(lambda e, gt=gt, nch=nch, bk=bk:
                                 e.activation(
                    vtok[:, gt * D + nch * 512: gt * D + (nch + 1) * 512],
                    bk.ap, AF.Relu), inc=(cA, 1))
                bk.free = (cA, last_v_av)

        VG = [(tt, nch) for tt in range(ACH // P) for nch in range(2)]
        for tc in range(NACH):
            sl = slice(tc * ACH, (tc + 1) * ACH)
            st_a = emit_ln_p1a(ACH, xbp, xsq, casts[tc])
            if vb_pending[0] is not None:
                emit_v(vb_pending[0], VG[4:])
                vb_pending[0] = None
            st = emit_ln_p1b(st_a)
            st = (st[0], st[1], st[2], cP.v)
            plan.wait("DVE", qS, z_store[tc])
            zdones[tc] = emit_ln_p2(ACH, st, xc, zc, tmp=t1)
            plan.wait("SP", cD, zdones[tc])
            z_store[tc + 1] = SPo(lambda e, sl=sl, b=b: e.dma_start(
                zd.ap()[b, :, :, sl].rearrange("t p f -> p t f"),
                zc[:, :].rearrange("p (t f) -> p t f", t=DT)), inc=(qS, 16))
            emit_k(tc)
            if tc + 1 < NACH:
                a1_load(tc + 1)
                a1_casts(tc + 1)
            emit_v(tc, VG[:4])
            vb_pending[0] = tc
        emit_v(vb_pending[0], VG[4:])
        vb_pending[0] = None

        plan.wait("SP", cP, cP.v)
        plan.wait("SP", cD, cD.v)
        plan.wait("SP", qS, qS.v)
        z_load = SPo(lambda e, b=b: e.dma_start(
            zc[:, :].rearrange("p (t f) -> p t f", t=DT),
            zd.ap()[b, :, :, 0:ACH].rearrange("t p f -> p t f")),
            inc=(qX, 16))
        z3_store_prev = 0
        x2_store_prev = 0
        xcr = None

        def emit_qproj(qcn, zl):
            plan.wait("PE", qX, zl)
            plan.wait("ACT", qS, z3_store_prev)
            return emit_proj(wq_s, zc, ACH, lambda mt: qc_s[:, ts(mt, ACH)],
                             bq_s, AF.Relu, w_rdy["wq"], None)

        q_ready = emit_qproj(0, z_load)
        plan.wait("SP", cP, cP.v)
        z_load = SPo(lambda e, b=b: e.dma_start(
            zc[:, :].rearrange("p (t f) -> p t f", t=DT),
            zd.ap()[b, :, :, ACH:2 * ACH].rearrange("t p f -> p t f")),
            inc=(qX, 16))

        for qcn in range(NACH):
            sl = slice(qcn * ACH, (qcn + 1) * ACH)
            nkc = qcn + 1
            nkt = 4 * nkc
            kw = nkc * 512
            p_avs = [None] * 4

            def emit_scores(r):
                sr = (srow, srow2)[r % 2]
                plan.wait("PE", cA, q_ready)
                for kc in range(nkc):
                    bk = next_bank()
                    bk.acquire(plan)
                    diag = (kc == nkc - 1)
                    pv = None
                    for dt in range(DT):
                        last_mm = (dt == DT - 1) and not diag
                        pv = PEo(lambda e, r=r, kc=kc, dt=dt, bk=bk,
                                 last_mm=last_mm: e.matmul(
                            bk.ap,
                            qc_s[:, dt * ACH + r * P: dt * ACH + (r + 1) * P],
                            kT[:, dt * SEQ + kc * 512:
                               dt * SEQ + (kc + 1) * 512],
                            start=(dt == 0), stop=last_mm),
                            inc=(cP, 1) if last_mm else None)
                    if diag:
                        pv = PEo(lambda e, r=r, bk=bk: e.matmul(
                            bk.ap, iden_s[:, :], mask_s[:, ts(r, 512)],
                            start=False, stop=True), inc=(cP, 1))
                    plan.wait("ACT", cP, pv)
                    if r % 2 == 1:
                        plan.wait("ACT", qS, z3_store_prev)
                    av = ACTo(lambda e, kc=kc, bk=bk, sr=sr: e.activation(
                        sr[:, ts(kc, 512)], bk.ap, AF.Copy), inc=(cA, 1))
                    bk.free = (cA, av)
                plan.wait("DVE", cA, cA.v)
                plan.wait("DVE", cD, cD.v)
                dv = DVEo(lambda e, kw=kw, sr=sr: e.reduce_max(
                    nmx[:, 0:1], sr[:, 0:kw], AX.X, negate=True), inc=(cD, 1))
                plan.wait("DVE", cD, dv)
                dv = DVEo(lambda e: e.tensor_scalar(
                    nmx2[:, 0:1], nmx[:, 0:1], ISD, None, ALU.mult),
                    inc=(cD, 1))
                plan.wait("ACT", cD, dv)
                av = ACTo(lambda e, kw=kw, sr=sr: e.activation(
                    xbp[:, 0:kw], sr[:, 0:kw], AF.Exp, bias=nmx2[:, 0:1],
                    scale=ISD, accum_out=esum[:, 0:1]), inc=(cA, 1))
                plan.wait("DVE", cA, av)
                dv = DVEo(lambda e: e.reciprocal(rcol[:, 0:1], esum[:, 0:1]),
                          inc=(cD, 1))
                plan.wait("ACT", cD, dv)
                po = (r % 2) * 2048
                p_avs[r] = ACTo(lambda e, kw=kw, po=po: e.activation(
                    xsq[:, po:po + kw], xbp[:, 0:kw], AF.Copy,
                    scale=rcol[:, 0:1]), inc=(cA, 1))

            def emit_transp(r):
                po = (r % 2) * 2048
                plan.wait("PE", cA, p_avs[r])
                for w0 in range(0, nkt, 8):
                    tp = (pAl, pAh)[(w0 // 8) % 2]
                    tp.acquire(plan)
                    n8 = min(8, nkt - w0)
                    for i in range(n8):
                        pv = PEo(lambda e, w0=w0, i=i, tp=tp, po=po:
                                 e.transpose(
                            tp.bf[:, ts(i, P)],
                            xsq[:, po + (w0 + i) * P: po + (w0 + i + 1) * P],
                            iden_s[:, :]),
                            inc=(cP, 1) if i == n8 - 1 else None)
                    plan.wait("DVE", cP, pv)
                    dv = None
                    for i in range(n8):
                        dv = DVEo(lambda e, w0=w0, i=i, r=r, tp=tp:
                                  e.tensor_copy(
                            ptb[:, (w0 + i) * ACH + r * P:
                                (w0 + i) * ACH + (r + 1) * P],
                            tp.bf[:, ts(i, P)]),
                            inc=(cD, 1) if i == n8 - 1 else None)
                    tp.free = (cD, dv)

            emit_scores(0)
            for r in range(1, 4):
                emit_scores(r)
                emit_transp(r - 1)
            emit_transp(3)
            pt_ready = cD.v

            plan.wait("SP", cA, cA.v)
            plan.wait("SP", cD, cD.v)
            plan.wait("SP", qS, qS.v)
            if xcr is not None:
                plan.wait("SP", cA, xcr[0])
                plan.wait("SP", cD, xcr[1])
            xl = SPo(lambda e, sl=sl, b=b: e.dma_start(
                xc[:, :].rearrange("p (t f) -> p t f", t=DT),
                xTd.ap()[b, :, :, sl].rearrange("t p f -> p t f")),
                inc=(qX, 16))

            plan.wait("PE", cD, pt_ready)
            y_ready = None
            for dm in range(DT):
                bk = next_bank()
                bk.acquire(plan)
                for kt in range(nkt):
                    pv = PEo(lambda e, dm=dm, kt=kt, bk=bk, nkt=nkt: e.matmul(
                        bk.ap, vtok[:, kt * D + dm * P: kt * D + (dm + 1) * P],
                        ptb[:, ts(kt, ACH)], start=(kt == 0),
                        stop=(kt == nkt - 1)),
                        inc=(cP, 1) if kt == nkt - 1 else None)
                plan.wait("ACT", cP, pv)
                y_ready = ACTo(lambda e, dm=dm, bk=bk: e.activation(
                    qc_s[:, ts(dm, ACH)], bk.ap, AF.Copy), inc=(cA, 1))
                bk.free = (cA, y_ready)
            plan.wait("PE", cA, y_ready)
            ao_ready = emit_proj(wo_s, qc_s, ACH,
                                 lambda mt: xbp[:, ts(mt, ACH)],
                                 bo_s, AF.Relu, w_rdy["wo"], None)
            if qcn == NACH - 1:
                f_gate = [(cP, cP.v), (cA, cA.v), (cD, cD.v)]
            plan.wait("ACT", cA, ao_ready)
            plan.wait("ACT", cP, cP.v)
            cast_av = ACTo(lambda e: e.activation(xsq[:, :], xbp[:, :],
                                                  AF.Square), inc=(cA, 1))
            st2 = emit_ln_p1(ACH, xbp, xsq, cast_av)
            if qcn + 1 < NACH:
                q_ready = emit_qproj(qcn + 1, z_load)
                if qcn + 2 < NACH:
                    plan.wait("SP", cP, cP.v)
                    nsl = slice((qcn + 2) * ACH, (qcn + 3) * ACH)
                    z_load = SPo(lambda e, nsl=nsl, b=b: e.dma_start(
                        zc[:, :].rearrange("p (t f) -> p t f", t=DT),
                        zd.ap()[b, :, :, nsl].rearrange("t p f -> p t f")),
                        inc=(qX, 16))
            plan.wait("DVE", qX, xl)

            def apply_res(slz, Rap, Aap, last):
                DVEo(lambda e, slz=slz, Rap=Rap: e.tensor_mul(
                    t1[:, 0:512], xbp[:, slz], Rap))
                DVEo(lambda e, Aap=Aap: e.tensor_add(
                    t2[:, 0:512], t1[:, 0:512], Aap))
                return DVEo(lambda e, slz=slz: e.tensor_add(
                    xc[:, slz], xc[:, slz], t2[:, 0:512]),
                    inc=(cD, 1) if last else None)

            res_dv = emit_ln_p2(ACH, st2, xbp, None, apply_cb=apply_res)
            plan.wait("SP", cD, res_dv)
            x2_store_prev = SPo(lambda e, sl=sl, b=b: e.dma_start(
                x2d.ap()[b, :, :, sl].rearrange("t p f -> p t f"),
                xc[:, :].rearrange("p (t f) -> p t f", t=DT)), inc=(qS, 16))

            plan.wait("ACT", cD, res_dv)
            ACTo(lambda e: e.activation(xbp[:, :], xc[:, :], AF.Copy))
            cast_av = ACTo(lambda e: e.activation(xsq[:, :], xc[:, :],
                                                  AF.Square), inc=(cA, 1))
            st3 = emit_ln_p1(ACH, xbp, xsq, cast_av)
            plan.wait("DVE", qS, x2_store_prev)
            z3_dv = emit_ln_p2(ACH, st3, xc, z3c, tmp=t1)
            xcr = (cast_av, z3_dv)
            plan.wait("SP", cD, z3_dv)
            z3_store_prev = SPo(lambda e, sl=sl, b=b: e.dma_start(
                z3d.ap()[b, :, :, sl].rearrange("t p f -> p t f"),
                z3c[:, :].rearrange("p (t f) -> p t f", t=DT)), inc=(qS, 16))
            if qcn == 1:
                z3st_early = z3_store_prev

        a2_end_cd = cD.v
        a2_end_qs = qS.v

        for s2, v2 in f_gate:
            plan.wait("PL", s2, v2)

        def stream_slab(dsth, src_d, mt):
            if id(dsth) in dst_prev_users:
                plan.wait("PL", *dst_prev_users[id(dsth)])
            return PLo(lambda g, d=dsth, s=src_d, mt=mt: g.dma_start(
                d[:, :], s.ap()[mt]), inc=(qW, 16))

        z3l = None
        for fc in range(NFCH):
            fsl = slice(fc * FCH, (fc + 1) * FCH)
            if fc == 0:
                for s2, v2 in f_gate:
                    plan.wait("SP", s2, v2)
                plan.wait("SP", cD, a2_end_cd)
                plan.wait("SP", qS, z3st_early)
                z3l = SPo(lambda e, fsl=fsl, b=b: e.dma_start(
                    z3[:, :].rearrange("p (t f) -> p t f", t=DT),
                    z3d.ap()[b, :, :, fsl].rearrange("t p f -> p t f")),
                    inc=(qX, 16))

            plan.wait("PE", qX, z3l)
            a_last = None
            for mt in range(HT):
                k1, k2 = wsl[2 * (mt % 2)], wsl[2 * (mt % 2) + 1]
                r1 = stream_slab(k1, w1_d, mt)
                r2 = stream_slab(k2, w2_d, mt)
                plan.wait("PE", qW, r2)
                pvU = pvV = None
                for j in range(2):
                    bkU = next_bank()
                    bkU.acquire(plan)
                    for kt in range(DT):
                        pvU = PEo(lambda e, j=j, kt=kt, bk=bkU, w=k1: e.matmul(
                            bk.ap, w[:, ts(kt, P)],
                            z3[:, kt * FCH + j * 512: kt * FCH + (j + 1) * 512],
                            start=(kt == 0), stop=(kt == DT - 1)))
                    bkV = next_bank()
                    bkV.acquire(plan)
                    for kt in range(DT):
                        pvV = PEo(lambda e, j=j, kt=kt, bk=bkV, w=k2: e.matmul(
                            bk.ap, w[:, ts(kt, P)],
                            z3[:, kt * FCH + j * 512: kt * FCH + (j + 1) * 512],
                            start=(kt == 0), stop=(kt == DT - 1)),
                            inc=(cP, 1) if kt == DT - 1 else None)
                    uu = u_s[uflip[0] % 4]
                    vv = v2_s[uflip[0] % 4]
                    uflip[0] += 1
                    pvU = pvV
                    plan.wait("ACT", cP, pvV)
                    if id(uu) in dst_prev_users:
                        plan.wait("ACT", *dst_prev_users[id(uu)])
                    av = ACTo(lambda e, mt=mt, bk=bkU, uu=uu: e.activation(
                        uu[:, :], bk.ap, AF.Silu, scale=s1_s[:, mt:mt + 1]),
                        inc=(cA, 1))
                    bkU.free = (cA, av)
                    av = ACTo(lambda e, mt=mt, bk=bkV, vv=vv: e.activation(
                        vv[:, :], bk.ap, AF.Copy, scale=s2_s[:, mt:mt + 1]),
                        inc=(cA, 1))
                    bkV.free = (cA, av)
                    plan.wait("DVE", cA, av)
                    if mt == 0 and j == 0:
                        plan.wait("DVE", qS, qS.v)
                    a_last = DVEo(lambda e, mt=mt, j=j, uu=uu, vv=vv:
                                  e.tensor_mul(
                        abuf[:, mt * FCH + j * 512: mt * FCH + (j + 1) * 512],
                        uu[:, :], vv[:, :]), inc=(cD, 1))
                    dst_prev_users[id(uu)] = (cD, a_last)
                dst_prev_users[id(k1)] = (cP, pvU)
                dst_prev_users[id(k2)] = (cP, pvV)
            a_ready = a_last

            def h_stage(src_h, dst_h, h_dd, hs_col, rdy_sem, rdy_val):
                plan.wait("PE", rdy_sem, rdy_val)
                last_av = None
                pv = None
                for mt in range(HT):
                    slab = hsl[mt % 2]
                    rr = stream_slab(slab, h_dd, mt)
                    plan.wait("PE", qW, rr)
                    bks = []
                    for j in range(2):
                        bk = next_bank()
                        bk.acquire(plan)
                        bks.append(bk)
                        for kt in range(HT):
                            pv = PEo(lambda e, j=j, kt=kt, bk=bk, sh=src_h,
                                     slab=slab: e.matmul(
                                bk.ap, slab[:, ts(kt, P)],
                                sh[:, kt * FCH + j * 512:
                                   kt * FCH + (j + 1) * 512],
                                start=(kt == 0), stop=(kt == HT - 1)),
                                inc=(cP, 1)
                                if (kt == HT - 1 and j == 1) else None)
                    plan.wait("ACT", cP, pv)
                    for j in range(2):
                        last_av = ACTo(lambda e, mt=mt, j=j, bk=bks[j],
                                       dh=dst_h, hc=hs_col: e.activation(
                            dh[:, mt * FCH + j * 512: mt * FCH + (j + 1) * 512],
                            bk.ap, AF.Gelu, scale=hc[:, mt:mt + 1]),
                            inc=(cA, 1))
                        bks[j].free = (cA, last_av)
                    dst_prev_users[id(slab)] = (cP, pv)
                return last_av

            b_ready = h_stage(abuf, bbuf, h1_d, h1s_s, cD, a_ready)
            c_ready = h_stage(bbuf, abuf, h2_d, h2s_s, cA, b_ready)

            plan.wait("PE", cA, c_ready)
            fdone = None
            w3_pv = None
            for mt in range(DT):
                slab = hsl[mt % 2]
                rr = stream_slab(slab, w3_d, mt)
                plan.wait("PE", qW, rr)
                for j in range(2):
                    bk = next_bank()
                    bk.acquire(plan)
                    for kt in range(HT):
                        w3_pv = PEo(lambda e, j=j, kt=kt, bk=bk, slab=slab:
                                    e.matmul(
                            bk.ap, slab[:, ts(kt, P)],
                            abuf[:, kt * FCH + j * 512:
                                 kt * FCH + (j + 1) * 512],
                            start=(kt == 0), stop=(kt == HT - 1)),
                            inc=(cP, 1) if kt == HT - 1 else None)
                    plan.wait("ACT", cP, w3_pv)
                    fdone = ACTo(lambda e, mt=mt, j=j, bk=bk: e.activation(
                        fbuf[:, mt * FCH + j * 512: mt * FCH + (j + 1) * 512],
                        bk.ap, AF.Copy, scale=w3s_s[:, mt:mt + 1]),
                        inc=(cA, 1))
                    bk.free = (cA, fdone)
                dst_prev_users[id(slab)] = (cP, w3_pv)

            if fc + 1 < NFCH:
                plan.wait("SP", cP, cP.v)
                plan.wait("SP", qS, a2_end_qs)
                nfsl = slice((fc + 1) * FCH, (fc + 2) * FCH)
                z3l = SPo(lambda e, nfsl=nfsl, b=b: e.dma_start(
                    z3[:, :].rearrange("p (t f) -> p t f", t=DT),
                    z3d.ap()[b, :, :, nfsl].rearrange("t p f -> p t f")),
                    inc=(qX, 16))

            plan.wait("SP", cP, w3_pv)
            x2l2 = SPo(lambda e, fsl=fsl, b=b: e.dma_start(
                x2c2[:, :].rearrange("p (t f) -> p t f", t=DT),
                x2d.ap()[b, :, :, fsl].rearrange("t p f -> p t f")),
                inc=(qX, 16))

            plan.wait("ACT", cA, fdone)
            plan.wait("ACT", cD, a2_end_cd)
            plan.wait("ACT", qS, a2_end_qs)
            ACTo(lambda e: e.activation(fb16[:, :], fbuf[:, :], AF.Copy))
            cast_av = ACTo(lambda e: e.activation(fsq16[:, :], fbuf[:, :],
                                                  AF.Square), inc=(cA, 1))
            st4 = emit_ln_p1(FCH, fb16, fsq16, cast_av)
            if fc == NFCH - 1:
                prevF_ln4 = (cast_av, cP.v)
            plan.wait("DVE", qX, x2l2)

            def apply_res4(slz, Rap, Aap, last):
                DVEo(lambda e, slz=slz, Rap=Rap: e.tensor_mul(
                    t3[:, :], fbuf[:, slz], Rap))
                DVEo(lambda e, Aap=Aap: e.tensor_add(
                    t4[:, :], t3[:, :], Aap))
                return DVEo(lambda e, slz=slz: e.tensor_add(
                    x2c2[:, slz], x2c2[:, slz], t4[:, :]),
                    inc=(cD, 1) if last else None)

            res_dv = emit_ln_p2(FCH, st4, fbuf, None, apply_cb=apply_res4)
            plan.wait("SP", cD, res_dv)
            SPo(lambda e, fsl=fsl, b=b: e.dma_start(
                outd.ap()[b, :, :, fsl].rearrange("t p f -> p t f"),
                x2c2[:, :].rearrange("p (t f) -> p t f", t=DT)), inc=(qS, 16))

        attn_gate = [(cP, w3_pv)]
        prevF_cd = cD.v
        prevF_qs = qS.v

    plan.wait("SP", qS, qS.v)

    with nc.Block() as block:
        @block.tensor
        def _(eng):
            for f in plan.E["PE"]:
                f(eng)

        @block.scalar
        def _(eng):
            for f in plan.E["ACT"]:
                f(eng)

        @block.vector
        def _(eng):
            for f in plan.E["DVE"]:
                f(eng)

        @block.sync
        def _(eng):
            for f in plan.E["SP"]:
                f(eng)

        @block.gpsimd
        def _(eng):
            for f in plan.E["PL"]:
                f(eng)

    return nc



def _prep_weights(inp):
    import ml_dtypes as mld
    f32 = np.float32

    def lhsT_pack(w_din_dout, dt_in, dt_out):
        w = np.ascontiguousarray(np.asarray(w_din_dout, f32))
        t = w.reshape(dt_in, P, dt_out, P).transpose(1, 2, 0, 3)
        return np.ascontiguousarray(t.reshape(P, dt_out * dt_in * P))

    def slab_pack(w_din_dout, dt_in, dt_out):
        w = np.ascontiguousarray(np.asarray(w_din_dout, f32))
        t = w.reshape(dt_in, P, dt_out, P).transpose(2, 1, 0, 3)
        return np.ascontiguousarray(t.reshape(dt_out, P, dt_in * P))

    def col(v, nt):
        return np.ascontiguousarray(np.asarray(v, f32).reshape(nt, P).T)

    mask = np.zeros((P, NACH * ACH), dtype=f32)
    i = np.arange(P)[:, None]
    j = np.arange(ACH)[None, :]
    for v in range(NACH):
        mask[:, v * ACH:(v + 1) * ACH] = np.where(j > i + P * v, MASKVAL, 0.0)

    return {
        "wq": lhsT_pack(inp["L0_attn_Wq"], DT, DT),
        "wk": lhsT_pack(inp["L0_attn_Wk"], DT, DT),
        "wo": lhsT_pack(inp["L0_attn_Wo"], DT, DT),
        "wv": np.ascontiguousarray(
            np.asarray(inp["L0_attn_Wv"], f32).reshape(DT, P, D)
            .transpose(1, 0, 2).reshape(P, DT * D)),
        "w1p": slab_pack(np.asarray(inp["L0_ffn_w1_w"], f32).T, DT, HT),
        "w2p": slab_pack(np.asarray(inp["L0_ffn_w2_w"], f32).T, DT, HT),
        "h1p": slab_pack(np.asarray(inp["L0_ffn_h1_w"], f32).T, HT, HT),
        "h2p": slab_pack(np.asarray(inp["L0_ffn_h2_w"], f32).T, HT, HT),
        "w3p": slab_pack(np.asarray(inp["L0_ffn_w3_w"], f32).T, HT, DT),
        "bq": col(inp["L0_attn_bq"], DT),
        "bk": col(inp["L0_attn_bk"], DT),
        "bo": col(inp["L0_attn_bo"], DT),
        "bvb": np.ascontiguousarray(
            np.asarray(inp["L0_attn_bv"], f32).reshape(1, D)),
        "s1c": col(inp["L0_ffn_w1_s"], HT),
        "s2c": col(inp["L0_ffn_w2_s"], HT),
        "h1sc": col(inp["L0_ffn_h1_s"], HT),
        "h2sc": col(inp["L0_ffn_h2_s"], HT),
        "w3sc": col(inp["L0_ffn_w3_s"], DT),
        "mask": mask,
        "iden": np.eye(P, dtype=mld.bfloat16),
        "onesc": np.ones((P, 1), dtype=mld.bfloat16),
        "onesr": np.ones((1, P), dtype=np.float32),
    }


def kernel(**inputs):
    x = np.asarray(inputs["L0_attn_x"], dtype=np.float32)
    shared = _prep_weights(inputs)
    nc = build_program()
    in_maps = []
    for c in range(NCORES):
        m = dict(shared)
        m["xT"] = np.ascontiguousarray(
            x[c * NB:(c + 1) * NB].transpose(0, 2, 1)).reshape(NB, DT, P, SEQ)
        in_maps.append(m)
    res = run_bass_kernel_spmd(nc, in_maps, core_ids=list(range(NCORES)))
    out = np.empty((NCORES * NB, SEQ, D), dtype=np.float32)
    for c in range(NCORES):
        y = np.asarray(res.results[c]["outT"])
        out[c * NB:(c + 1) * NB] = y.reshape(NB, D, SEQ).transpose(0, 2, 1)
    return out


# version 14
# speedup vs baseline: 1.0727x; 1.0465x over previous
import numpy as np
from contextlib import ExitStack

import concourse.bass as bass
import concourse.mybir as mybir
from concourse.bass_utils import run_bass_kernel_spmd

F32 = mybir.dt.float32
BF16 = mybir.dt.bfloat16
AF = mybir.ActivationFunctionType
ALU = mybir.AluOpType
AX = mybir.AxisListType

P = 128
D = 1024
DT = D // P
H = 4096
HT = H // P
SEQ = 2048
NB = 4
NCORES = 8
ACH = 512
FCH = 1024
NACH = SEQ // ACH
NFCH = SEQ // FCH
MASKVAL = -320000.0
ISD = 1.0 / 32.0
EPS = 1e-5
KiB = 1024


def ts(i, n):
    return slice(i * n, (i + 1) * n)


class Sem:
    def __init__(self, handle, name):
        self.h = handle
        self.name = name
        self.v = 0


class Plan:
    ENGINES = ("PE", "ACT", "DVE", "SP", "PL")

    def __init__(self):
        self.E = {k: [] for k in self.ENGINES}
        self.waited = {k: {} for k in self.ENGINES}

    def wait(self, en, sem, val):
        if val is None or val <= 0:
            return
        if self.waited[en].get(sem.name, 0) >= val:
            return
        self.waited[en][sem.name] = val
        self.E[en].append(lambda eng, s=sem.h, v=val: eng.wait_ge(s, v))

    def op(self, en, fn, inc=None):
        if inc is not None:
            sem, n = inc
            sem.v += n
            tgt = sem.v
            self.E[en].append(
                lambda eng, fn=fn, s=sem.h, n=n: fn(eng).then_inc(s, n))
            return tgt
        self.E[en].append(lambda eng, fn=fn: fn(eng))
        return None


class PBank:
    def __init__(self, ap, bf=None):
        self.ap = ap
        self.bf = bf
        self.free = None

    def acquire(self, plan, en="PE"):
        if self.free is not None:
            plan.wait(en, *self.free)


def build_program():
    nc = bass.Bass('TRN2', target_bir_lowering=False)

    xTd = nc.dram_tensor("xT", [NB, DT, P, SEQ], F32, kind="ExternalInput")
    outd = nc.dram_tensor("outT", [NB, DT, P, SEQ], F32, kind="ExternalOutput")
    x2d = nc.dram_tensor("x2scr", [NB, DT, P, SEQ], F32, kind="Internal")
    zd = nc.dram_tensor("zscr", [NB, DT, P, SEQ], BF16, kind="Internal")
    z3d = nc.dram_tensor("z3scr", [NB, DT, P, SEQ], BF16, kind="Internal")

    wq_d = nc.dram_tensor("wq", [P, DT * DT * P], F32, kind="ExternalInput")
    wk_d = nc.dram_tensor("wk", [P, DT * DT * P], F32, kind="ExternalInput")
    wo_d = nc.dram_tensor("wo", [P, DT * DT * P], F32, kind="ExternalInput")
    wv_d = nc.dram_tensor("wv", [P, DT * D], F32, kind="ExternalInput")
    w1_d = nc.dram_tensor("w1p", [HT, P, DT * P], F32, kind="ExternalInput")
    w2_d = nc.dram_tensor("w2p", [HT, P, DT * P], F32, kind="ExternalInput")
    h1_d = nc.dram_tensor("h1p", [HT, P, HT * P], F32, kind="ExternalInput")
    h2_d = nc.dram_tensor("h2p", [HT, P, HT * P], F32, kind="ExternalInput")
    w3_d = nc.dram_tensor("w3p", [DT, P, HT * P], F32, kind="ExternalInput")
    bq_d = nc.dram_tensor("bq", [P, DT], F32, kind="ExternalInput")
    bk_d = nc.dram_tensor("bk", [P, DT], F32, kind="ExternalInput")
    bo_d = nc.dram_tensor("bo", [P, DT], F32, kind="ExternalInput")
    bvb_d = nc.dram_tensor("bvb", [1, D], F32, kind="ExternalInput")
    s1_d = nc.dram_tensor("s1c", [P, HT], F32, kind="ExternalInput")
    s2_d = nc.dram_tensor("s2c", [P, HT], F32, kind="ExternalInput")
    h1s_d = nc.dram_tensor("h1sc", [P, HT], F32, kind="ExternalInput")
    h2s_d = nc.dram_tensor("h2sc", [P, HT], F32, kind="ExternalInput")
    w3s_d = nc.dram_tensor("w3sc", [P, DT], F32, kind="ExternalInput")
    mask_d = nc.dram_tensor("mask", [P, NACH * ACH], F32, kind="ExternalInput")
    iden_d = nc.dram_tensor("iden", [P, P], BF16, kind="ExternalInput")
    onesc_d = nc.dram_tensor("onesc", [P, 1], BF16, kind="ExternalInput")
    onesr_d = nc.dram_tensor("onesr", [1, P], F32, kind="ExternalInput")

    plan = Plan()
    PEo = lambda fn, inc=None: plan.op("PE", fn, inc)
    ACTo = lambda fn, inc=None: plan.op("ACT", fn, inc)
    DVEo = lambda fn, inc=None: plan.op("DVE", fn, inc)
    SPo = lambda fn, inc=None: plan.op("SP", fn, inc)
    PLo = lambda fn, inc=None: plan.op("PL", fn, inc)

    mksem = lambda n: Sem(nc.alloc_semaphore(name=n), n)
    qW = mksem("qW")
    qX = mksem("qX")
    qS = mksem("qS")
    cP = mksem("cP")
    cA = mksem("cA")
    cD = mksem("cD")

    sb = lambda name, shape, dtype: nc.alloc_sbuf_tensor(name, shape, dtype)
    iden_s = sb("iden_s", [P, P], BF16)
    onesc_s = sb("onesc_s", [P, 1], BF16)
    onesr_s = sb("onesr_s", [1, P], F32)
    eps_s = sb("eps_s", [1, 1], F32)
    bq_s = sb("bq_s", [P, DT], F32)
    bk_s = sb("bk_s", [P, DT], F32)
    bo_s = sb("bo_s", [P, DT], F32)
    bvb_s = sb("bvb_s", [1, D], F32)
    s1_s = sb("s1_s", [P, HT], F32)
    s2_s = sb("s2_s", [P, HT], F32)
    h1s_s = sb("h1s_s", [P, HT], F32)
    h2s_s = sb("h2s_s", [P, HT], F32)
    w3s_s = sb("w3s_s", [P, DT], F32)
    mask_s = sb("mask_s", [P, NACH * ACH], BF16)
    mrow = sb("mrow", [1, FCH], F32)
    vrow = sb("vrow", [1, FCH], F32)
    rrow = sb("rrow", [1, FCH], F32)
    arow = vrow
    nmx = sb("nmx", [P, 1], F32)
    nmx2 = sb("nmx2", [P, 1], F32)
    esum = sb("esum", [P, 1], F32)
    rcol = sb("rcol", [P, 1], F32)
    marker = nc.alloc_sbuf_tensor("arena_marker", [P, 32], mybir.dt.uint8)
    base = nc.lookup_mloc(marker).addr

    def at(name, shape, dtype, off):
        return nc.alloc_sbuf_tensor_at(name, shape, dtype, offset=base + off)

    wq_s = at("wq_s", [P, DT * DT * P], BF16, 0)
    wo_s = at("wo_s", [P, DT * DT * P], BF16, 16 * KiB)
    kT = at("kT", [P, DT * SEQ], BF16, 32 * KiB)
    vtok = at("vtok", [P, (SEQ // P) * D], BF16, 64 * KiB)
    xc = at("xc", [P, DT * ACH], F32, 96 * KiB)
    xbp = at("xbp", [P, DT * ACH], BF16, 112 * KiB)
    xsq = at("xsq", [P, DT * ACH], BF16, 120 * KiB)
    zc = at("zc", [P, DT * ACH], BF16, 128 * KiB)
    t1 = at("t1", [P, 512], F32, 136 * KiB)
    t2 = at("t2", [P, 512], F32, 138 * KiB)
    wk_s = at("wk_s", [P, DT * DT * P], BF16, 140 * KiB)
    wv_s = at("wv_s", [P, DT * D], BF16, 156 * KiB)
    qc_s = at("qc_s", [P, DT * ACH], BF16, 140 * KiB)
    srow = at("srow", [P, SEQ], F32, 148 * KiB)
    ptb = at("ptb", [P, (SEQ // P) * ACH], BF16, 156 * KiB)
    srow2 = at("srow2", [P, SEQ], F32, 96 * KiB)
    z3c = at("z3c", [P, DT * ACH], BF16, 96 * KiB)
    A_END = 172 * KiB

    abuf = at("abuf", [P, HT * FCH], BF16, 0)
    x2c2 = at("x2c2", [P, DT * FCH], F32, 0)
    fb16 = at("fb16", [P, DT * FCH], BF16, 96 * KiB)
    fsq16 = at("fsq16", [P, DT * FCH], BF16, 112 * KiB)
    bbuf = at("bbuf", [P, HT * FCH], BF16, 64 * KiB)
    fbuf = at("fbuf", [P, DT * FCH], F32, 64 * KiB)
    z3 = at("z3", [P, DT * FCH], BF16, 128 * KiB)
    wsl = [at(f"wsl{i}", [P, DT * P], BF16, 144 * KiB + i * 2 * KiB)
           for i in range(4)]
    hsl = [at(f"hsl{i}", [P, HT * P], BF16, 152 * KiB + i * 8 * KiB)
           for i in range(2)]
    t3 = at("t3", [P, 512], F32, 180 * KiB)
    t4 = at("t4", [P, 512], F32, 182 * KiB)
    u_s = [at(f"u{i}", [P, 512], BF16, 172 * KiB + i * KiB)
           for i in range(4)]
    v2_s = [at(f"v2{i}", [P, 512], BF16, 176 * KiB + i * KiB)
            for i in range(4)]
    F_END = 184 * KiB
    assert base + max(A_END, F_END) <= nc.sbuf_top, (
        base, A_END, F_END, nc.sbuf_top)

    pb_t = [nc.alloc_psum_tensor(f"pb{i}", [P, 512], F32) for i in range(4)]
    pR_t = nc.alloc_psum_tensor("pR", [P, 1024], F32)
    pA_t = nc.alloc_psum_tensor("pA", [P, 1024], F32)
    pbanks = [PBank(h[:, :]) for h in pb_t]
    pA_bf = pA_t.ap().bitcast(BF16)
    pRl = PBank(pR_t[:, 0:512])
    pRh = PBank(pR_t[:, 512:1024])
    pAl = PBank(pA_t[:, 0:512], pA_bf[:, 0:1024])
    pAh = PBank(pA_t[:, 512:1024], pA_bf[:, 1024:2048])
    rot = [0]

    def next_bank():
        bk = pbanks[rot[0] % 4]
        rot[0] += 1
        return bk

    PLo(lambda g: g.memset(eps_s[0:1, 0:1], EPS), inc=(qW, 1))
    for dst, src in ((iden_s, iden_d), (onesc_s, onesc_d), (onesr_s, onesr_d),
                     (bq_s, bq_d), (bk_s, bk_d), (bo_s, bo_d), (bvb_s, bvb_d),
                     (s1_s, s1_d), (s2_s, s2_d), (h1s_s, h1s_d),
                     (h2s_s, h2s_d), (w3s_s, w3s_d), (mask_s, mask_d)):
        PLo(lambda g, d=dst, s=src: g.dma_start(d[:, :], s.ap()), inc=(qW, 16))
    for en in ("PE", "ACT", "DVE", "SP"):
        plan.wait(en, qW, qW.v)

    def emit_ln_p1a(width, xb_h, xq_h, ready_act):
        nw = width // 512
        plan.wait("PE", cA, ready_act)
        if nw == 1:
            stat = [pAl, pAh]
        else:
            stat = [next_bank() for _ in range(2 * nw)]
        pv = None
        for j in range(nw):
            for idx, sh in ((0, xb_h), (1, xq_h)):
                bk = stat[2 * j + idx]
                bk.acquire(plan)
                for t in range(DT):
                    pv = PEo(lambda e, j=j, t=t, bk=bk, sh=sh: e.matmul(
                        bk.ap[0:1, :], onesc_s[:, 0:1],
                        sh[:, t * width + j * 512: t * width + (j + 1) * 512],
                        start=(t == 0), stop=(t == DT - 1)),
                        inc=(cP, 1) if t == DT - 1 else None)
        plan.wait("DVE", cP, pv)
        dv = None
        for j in range(nw):
            DVEo(lambda e, j=j, bk=stat[2 * j]: e.tensor_scalar(
                mrow[0:1, ts(j, 512)], bk.ap[0:1, :], 1.0 / D, None, ALU.mult))
            DVEo(lambda e, j=j: e.tensor_mul(
                vrow[0:1, ts(j, 512)], mrow[0:1, ts(j, 512)],
                mrow[0:1, ts(j, 512)]))
            dv = DVEo(lambda e, j=j, bk=stat[2 * j + 1]: e.scalar_tensor_tensor(
                vrow[0:1, ts(j, 512)], bk.ap[0:1, :], 1.0 / D,
                vrow[0:1, ts(j, 512)], ALU.mult, ALU.subtract), inc=(cD, 1))
        for s in stat:
            if s.free is None or s.free[0] is not cD or s.free[1] < dv:
                s.free = (cD, dv)
        plan.wait("ACT", cD, dv)
        ACTo(lambda e: e.activation(rrow[0:1, 0:width], vrow[0:1, 0:width],
                                    AF.Ln, bias=eps_s[0:1, 0:1]))
        av = ACTo(lambda e: e.activation(rrow[0:1, 0:width], rrow[0:1, 0:width],
                                         AF.Exp, scale=-0.5), inc=(cA, 1))
        plan.wait("DVE", cA, av)
        dv = DVEo(lambda e: e.scalar_tensor_tensor(
            arow[0:1, 0:width], mrow[0:1, 0:width], -1.0, rrow[0:1, 0:width],
            ALU.mult, ALU.mult), inc=(cD, 1))
        return (nw, dv)

    def emit_ln_p1b(st_a):
        nw, dv = st_a
        plan.wait("PE", cD, dv)
        Rb = [pRl, pRh][:nw]
        Ab = [pAl, pAh][:nw]
        pv = None
        for j in range(nw):
            Rb[j].acquire(plan)
            pv = PEo(lambda e, j=j, bk=Rb[j]: e.matmul(
                bk.ap, onesr_s[0:1, :], rrow[0:1, ts(j, 512)],
                start=True, stop=True), inc=(cP, 1))
        for j in range(nw):
            Ab[j].acquire(plan)
            pv = PEo(lambda e, j=j, bk=Ab[j]: e.matmul(
                bk.ap, onesr_s[0:1, :], arow[0:1, ts(j, 512)],
                start=True, stop=True), inc=(cP, 1))
        return (nw, Rb, Ab, pv)

    def emit_ln_p1(width, xb_h, xq_h, ready_act):
        return emit_ln_p1b(emit_ln_p1a(width, xb_h, xq_h, ready_act))

    def emit_ln_p2(width, st, src_h, dst_h, apply_cb=None, tmp=None):
        nw, Rb, Ab, pv = st
        plan.wait("DVE", cP, pv)
        dv = None
        for t in range(DT):
            for j in range(nw):
                slz = slice(t * width + j * 512, t * width + (j + 1) * 512)
                last = (t == DT - 1 and j == nw - 1)
                if apply_cb is None:
                    DVEo(lambda e, slz=slz, j=j, tmp=tmp: e.tensor_mul(
                        tmp[:, 0:512], src_h[:, slz], Rb[j].ap))
                    dv = DVEo(lambda e, slz=slz, j=j, tmp=tmp: e.tensor_add(
                        dst_h[:, slz], tmp[:, 0:512], Ab[j].ap),
                        inc=(cD, 1) if last else None)
                else:
                    dv = apply_cb(slz, Rb[j].ap, Ab[j].ap, last)
        for j in range(nw):
            Rb[j].free = (cD, dv)
            Ab[j].free = (cD, dv)
        return dv

    def emit_ln(width, xb_h, xq_h, src_h, dst_h, ready_act, apply_cb=None,
                tmp=None):
        st = emit_ln_p1(width, xb_h, xq_h, ready_act)
        return emit_ln_p2(width, st, src_h, dst_h, apply_cb, tmp)

    def emit_proj(w_h, z_h, width, dst_fn, bias_col, func, w_ready, z_rdy_cD):
        plan.wait("PE", cD, z_rdy_cD)
        plan.wait("PE", qW, w_ready)
        av = None
        for mt in range(DT):
            bk = next_bank()
            bk.acquire(plan)
            pv = None
            for kt in range(DT):
                pv = PEo(lambda e, mt=mt, kt=kt, bk=bk: e.matmul(
                    bk.ap, w_h[:, (mt * DT + kt) * P:(mt * DT + kt + 1) * P],
                    z_h[:, ts(kt, width)], start=(kt == 0),
                    stop=(kt == DT - 1)), inc=(cP, 1) if kt == DT - 1 else None)
            plan.wait("ACT", cP, pv)
            if bias_col is not None:
                av = ACTo(lambda e, mt=mt, bk=bk: e.activation(
                    dst_fn(mt), bk.ap, func, bias=bias_col[:, mt:mt + 1]),
                    inc=(cA, 1))
            else:
                av = ACTo(lambda e, mt=mt, bk=bk: e.activation(
                    dst_fn(mt), bk.ap, func), inc=(cA, 1))
            bk.free = (cA, av)
        return av

    prev_gate = []
    dst_prev_users = {}
    attn_gate = []
    prevF_cd = 0
    prevF_qs = 0
    prevF_ln4 = None
    uflip = [0]

    for b in range(NB):
        for s2, v2 in attn_gate:
            plan.wait("PL", s2, v2)
        w_rdy = {}
        for nm, dsth, srcd in (("wk", wk_s, wk_d), ("wv", wv_s, wv_d),
                               ("wq", wq_s, wq_d), ("wo", wo_s, wo_d)):
            if nm == "wq":
                plan.wait("PL", cD, prevF_cd)
                plan.wait("PL", qS, prevF_qs)
            if id(dsth) in dst_prev_users:
                plan.wait("PL", *dst_prev_users[id(dsth)])
            w_rdy[nm] = PLo(lambda g, d=dsth, s=srcd: g.dma_start(
                d[:, :], s.ap()), inc=(qW, 16))

        plan.wait("ACT", cD, prevF_cd)
        plan.wait("ACT", qS, prevF_qs)

        xls = [None] * NACH
        casts = [None] * NACH
        z_store = [0] * (NACH + 1)

        def a1_load(tc):
            if tc == 0 and prevF_ln4 is not None:
                plan.wait("SP", cP, prevF_ln4[1])
                plan.wait("SP", cA, prevF_ln4[0])
            else:
                plan.wait("SP", cP, cP.v)
                plan.wait("SP", cA, cA.v)
                plan.wait("SP", cD, cD.v)
            sl = slice(tc * ACH, (tc + 1) * ACH)
            xls[tc] = SPo(lambda e, sl=sl, b=b: e.dma_start(
                xc[:, :].rearrange("p (t f) -> p t f", t=DT),
                xTd.ap()[b, :, :, sl].rearrange("t p f -> p t f")),
                inc=(qX, 16))

        def a1_casts(tc):
            plan.wait("ACT", qX, xls[tc])
            if tc == 0 and prevF_ln4 is not None:
                plan.wait("ACT", cP, prevF_ln4[1])
            else:
                plan.wait("ACT", cP, cP.v)
            ACTo(lambda e: e.activation(xbp[:, :], xc[:, :], AF.Copy))
            casts[tc] = ACTo(lambda e: e.activation(
                xsq[:, :], xc[:, :], AF.Square), inc=(cA, 1))

        a1_load(0)
        a1_casts(0)
        last_v_av = None
        zdones = [None] * NACH
        vb_pending = [None]

        def emit_k(tc):
            emit_proj(wk_s, zc, ACH,
                      lambda mt, tc=tc: kT[:, mt * SEQ + tc * ACH:
                                           mt * SEQ + (tc + 1) * ACH],
                      bk_s, AF.Relu, w_rdy["wk"], zdones[tc])

        def emit_v(tc, groups):
            nonlocal last_v_av
            plan.wait("PE", qW, w_rdy["wv"])
            for tt, nch in groups:
                bk = next_bank()
                bk.acquire(plan)
                for kt in range(DT):
                    PEo(lambda e, tt=tt, nch=nch, kt=kt, bk=bk: e.matmul(
                        bk.ap,
                        zc[:, kt * ACH + tt * P: kt * ACH + (tt + 1) * P],
                        wv_s[:, kt * D + nch * 512:
                             kt * D + (nch + 1) * 512],
                        start=(kt == 0), stop=False))
                pv = PEo(lambda e, nch=nch, bk=bk: e.matmul(
                    bk.ap, onesr_s[0:1, :], bvb_s[0:1, ts(nch, 512)],
                    start=False, stop=True), inc=(cP, 1))
                plan.wait("ACT", cP, pv)
                gt = tc * (ACH // P) + tt
                last_v_av = ACTo(lambda e, gt=gt, nch=nch, bk=bk:
                                 e.activation(
                    vtok[:, gt * D + nch * 512: gt * D + (nch + 1) * 512],
                    bk.ap, AF.Relu), inc=(cA, 1))
                bk.free = (cA, last_v_av)

        VG = [(tt, nch) for tt in range(ACH // P) for nch in range(2)]
        for tc in range(NACH):
            sl = slice(tc * ACH, (tc + 1) * ACH)
            st_a = emit_ln_p1a(ACH, xbp, xsq, casts[tc])
            if vb_pending[0] is not None:
                emit_v(vb_pending[0], VG[4:])
                vb_pending[0] = None
            st = emit_ln_p1b(st_a)
            st = (st[0], st[1], st[2], cP.v)
            plan.wait("DVE", qS, z_store[tc])
            zdones[tc] = emit_ln_p2(ACH, st, xc, zc, tmp=t1)
            plan.wait("SP", cD, zdones[tc])
            z_store[tc + 1] = SPo(lambda e, sl=sl, b=b: e.dma_start(
                zd.ap()[b, :, :, sl].rearrange("t p f -> p t f"),
                zc[:, :].rearrange("p (t f) -> p t f", t=DT)), inc=(qS, 16))
            emit_k(tc)
            if tc + 1 < NACH:
                a1_load(tc + 1)
                a1_casts(tc + 1)
            emit_v(tc, VG[:4])
            vb_pending[0] = tc
        emit_v(vb_pending[0], VG[4:])
        vb_pending[0] = None

        plan.wait("SP", cP, cP.v)
        plan.wait("SP", cD, cD.v)
        plan.wait("SP", qS, qS.v)
        z_load = SPo(lambda e, b=b: e.dma_start(
            zc[:, :].rearrange("p (t f) -> p t f", t=DT),
            zd.ap()[b, :, :, 0:ACH].rearrange("t p f -> p t f")),
            inc=(qX, 16))
        z3_store_prev = 0
        x2_store_prev = 0
        xcr = None

        def emit_qproj(qcn, zl):
            plan.wait("PE", qX, zl)
            plan.wait("ACT", qS, z3_store_prev)
            return emit_proj(wq_s, zc, ACH, lambda mt: qc_s[:, ts(mt, ACH)],
                             bq_s, AF.Relu, w_rdy["wq"], None)

        q_ready = emit_qproj(0, z_load)
        plan.wait("SP", cP, cP.v)
        z_load = SPo(lambda e, b=b: e.dma_start(
            zc[:, :].rearrange("p (t f) -> p t f", t=DT),
            zd.ap()[b, :, :, ACH:2 * ACH].rearrange("t p f -> p t f")),
            inc=(qX, 16))

        for qcn in range(NACH):
            sl = slice(qcn * ACH, (qcn + 1) * ACH)
            nkc = qcn + 1
            nkt = 4 * nkc
            kw = nkc * 512
            p_avs = [None] * 4

            def emit_scores(r):
                sr = (srow, srow2)[r % 2]
                plan.wait("PE", cA, q_ready)
                for kc in range(nkc):
                    bk = next_bank()
                    bk.acquire(plan)
                    diag = (kc == nkc - 1)
                    pv = None
                    for dt in range(DT):
                        last_mm = (dt == DT - 1) and not diag
                        pv = PEo(lambda e, r=r, kc=kc, dt=dt, bk=bk,
                                 last_mm=last_mm: e.matmul(
                            bk.ap,
                            qc_s[:, dt * ACH + r * P: dt * ACH + (r + 1) * P],
                            kT[:, dt * SEQ + kc * 512:
                               dt * SEQ + (kc + 1) * 512],
                            start=(dt == 0), stop=last_mm),
                            inc=(cP, 1) if last_mm else None)
                    if diag:
                        pv = PEo(lambda e, r=r, bk=bk: e.matmul(
                            bk.ap, iden_s[:, :], mask_s[:, ts(r, 512)],
                            start=False, stop=True), inc=(cP, 1))
                    plan.wait("ACT", cP, pv)
                    if r % 2 == 1:
                        plan.wait("ACT", qS, z3_store_prev)
                    av = ACTo(lambda e, kc=kc, bk=bk, sr=sr: e.activation(
                        sr[:, ts(kc, 512)], bk.ap, AF.Copy), inc=(cA, 1))
                    bk.free = (cA, av)
                plan.wait("DVE", cA, cA.v)
                plan.wait("DVE", cD, cD.v)
                dv = DVEo(lambda e, kw=kw, sr=sr: e.reduce_max(
                    nmx[:, 0:1], sr[:, 0:kw], AX.X, negate=True), inc=(cD, 1))
                plan.wait("DVE", cD, dv)
                dv = DVEo(lambda e: e.tensor_scalar(
                    nmx2[:, 0:1], nmx[:, 0:1], ISD, None, ALU.mult),
                    inc=(cD, 1))
                plan.wait("ACT", cD, dv)
                av = ACTo(lambda e, kw=kw, sr=sr: e.activation(
                    xbp[:, 0:kw], sr[:, 0:kw], AF.Exp, bias=nmx2[:, 0:1],
                    scale=ISD, accum_out=esum[:, 0:1]), inc=(cA, 1))
                plan.wait("DVE", cA, av)
                dv = DVEo(lambda e: e.reciprocal(rcol[:, 0:1], esum[:, 0:1]),
                          inc=(cD, 1))
                plan.wait("ACT", cD, dv)
                po = (r % 2) * 2048
                p_avs[r] = ACTo(lambda e, kw=kw, po=po: e.activation(
                    xsq[:, po:po + kw], xbp[:, 0:kw], AF.Copy,
                    scale=rcol[:, 0:1]), inc=(cA, 1))

            def emit_transp(r):
                po = (r % 2) * 2048
                plan.wait("PE", cA, p_avs[r])
                for w0 in range(0, nkt, 8):
                    tp = (pAl, pAh)[(w0 // 8) % 2]
                    tp.acquire(plan)
                    n8 = min(8, nkt - w0)
                    for i in range(n8):
                        pv = PEo(lambda e, w0=w0, i=i, tp=tp, po=po:
                                 e.transpose(
                            tp.bf[:, ts(i, P)],
                            xsq[:, po + (w0 + i) * P: po + (w0 + i + 1) * P],
                            iden_s[:, :]),
                            inc=(cP, 1) if i == n8 - 1 else None)
                    plan.wait("DVE", cP, pv)
                    dv = None
                    for i in range(n8):
                        dv = DVEo(lambda e, w0=w0, i=i, r=r, tp=tp:
                                  e.tensor_copy(
                            ptb[:, (w0 + i) * ACH + r * P:
                                (w0 + i) * ACH + (r + 1) * P],
                            tp.bf[:, ts(i, P)]),
                            inc=(cD, 1) if i == n8 - 1 else None)
                    tp.free = (cD, dv)

            emit_scores(0)
            for r in range(1, 4):
                emit_scores(r)
                emit_transp(r - 1)
            emit_transp(3)
            pt_ready = cD.v

            plan.wait("SP", cA, cA.v)
            plan.wait("SP", cD, cD.v)
            plan.wait("SP", qS, qS.v)
            if xcr is not None:
                plan.wait("SP", cA, xcr[0])
                plan.wait("SP", cD, xcr[1])
            xl = SPo(lambda e, sl=sl, b=b: e.dma_start(
                xc[:, :].rearrange("p (t f) -> p t f", t=DT),
                xTd.ap()[b, :, :, sl].rearrange("t p f -> p t f")),
                inc=(qX, 16))

            plan.wait("PE", cD, pt_ready)
            y_ready = None
            for dm in range(DT):
                bk = next_bank()
                bk.acquire(plan)
                for kt in range(nkt):
                    pv = PEo(lambda e, dm=dm, kt=kt, bk=bk, nkt=nkt: e.matmul(
                        bk.ap, vtok[:, kt * D + dm * P: kt * D + (dm + 1) * P],
                        ptb[:, ts(kt, ACH)], start=(kt == 0),
                        stop=(kt == nkt - 1)),
                        inc=(cP, 1) if kt == nkt - 1 else None)
                plan.wait("ACT", cP, pv)
                y_ready = ACTo(lambda e, dm=dm, bk=bk: e.activation(
                    qc_s[:, ts(dm, ACH)], bk.ap, AF.Copy), inc=(cA, 1))
                bk.free = (cA, y_ready)
            plan.wait("PE", cA, y_ready)
            ao_ready = emit_proj(wo_s, qc_s, ACH,
                                 lambda mt: xbp[:, ts(mt, ACH)],
                                 bo_s, AF.Relu, w_rdy["wo"], None)
            if qcn == NACH - 1:
                f_gate = [(cP, cP.v), (cA, cA.v), (cD, cD.v)]
            plan.wait("ACT", cA, ao_ready)
            plan.wait("ACT", cP, cP.v)
            cast_av = ACTo(lambda e: e.activation(xsq[:, :], xbp[:, :],
                                                  AF.Square), inc=(cA, 1))
            st2 = emit_ln_p1(ACH, xbp, xsq, cast_av)
            if qcn + 1 < NACH:
                q_ready = emit_qproj(qcn + 1, z_load)
                if qcn + 2 < NACH:
                    plan.wait("SP", cP, cP.v)
                    nsl = slice((qcn + 2) * ACH, (qcn + 3) * ACH)
                    z_load = SPo(lambda e, nsl=nsl, b=b: e.dma_start(
                        zc[:, :].rearrange("p (t f) -> p t f", t=DT),
                        zd.ap()[b, :, :, nsl].rearrange("t p f -> p t f")),
                        inc=(qX, 16))
            plan.wait("DVE", qX, xl)

            def apply_res(slz, Rap, Aap, last):
                DVEo(lambda e, slz=slz, Rap=Rap: e.tensor_mul(
                    t1[:, 0:512], xbp[:, slz], Rap))
                DVEo(lambda e, Aap=Aap: e.tensor_add(
                    t2[:, 0:512], t1[:, 0:512], Aap))
                return DVEo(lambda e, slz=slz: e.tensor_add(
                    xc[:, slz], xc[:, slz], t2[:, 0:512]),
                    inc=(cD, 1) if last else None)

            res_dv = emit_ln_p2(ACH, st2, xbp, None, apply_cb=apply_res)
            plan.wait("SP", cD, res_dv)
            x2_store_prev = SPo(lambda e, sl=sl, b=b: e.dma_start(
                x2d.ap()[b, :, :, sl].rearrange("t p f -> p t f"),
                xc[:, :].rearrange("p (t f) -> p t f", t=DT)), inc=(qS, 16))

            plan.wait("ACT", cD, res_dv)
            ACTo(lambda e: e.activation(xbp[:, :], xc[:, :], AF.Copy))
            cast_av = ACTo(lambda e: e.activation(xsq[:, :], xc[:, :],
                                                  AF.Square), inc=(cA, 1))
            st3 = emit_ln_p1(ACH, xbp, xsq, cast_av)
            plan.wait("DVE", qS, x2_store_prev)
            z3_dv = emit_ln_p2(ACH, st3, xc, z3c, tmp=t1)
            xcr = (cast_av, z3_dv)
            plan.wait("SP", cD, z3_dv)
            z3_store_prev = SPo(lambda e, sl=sl, b=b: e.dma_start(
                z3d.ap()[b, :, :, sl].rearrange("t p f -> p t f"),
                z3c[:, :].rearrange("p (t f) -> p t f", t=DT)), inc=(qS, 16))
            if qcn == 1:
                z3st_early = z3_store_prev

        a2_end_cd = cD.v
        a2_end_qs = qS.v

        for s2, v2 in f_gate:
            plan.wait("PL", s2, v2)

        def stream_slab(dsth, src_d, mt):
            if id(dsth) in dst_prev_users:
                plan.wait("PL", *dst_prev_users[id(dsth)])
            return PLo(lambda g, d=dsth, s=src_d, mt=mt: g.dma_start(
                d[:, :], s.ap()[mt]), inc=(qW, 16))

        z3l = None
        for fc in range(NFCH):
            fsl = slice(fc * FCH, (fc + 1) * FCH)
            if fc == 0:
                for s2, v2 in f_gate:
                    plan.wait("SP", s2, v2)
                plan.wait("SP", cD, a2_end_cd)
                plan.wait("SP", qS, z3st_early)
                z3l = SPo(lambda e, fsl=fsl, b=b: e.dma_start(
                    z3[:, :].rearrange("p (t f) -> p t f", t=DT),
                    z3d.ap()[b, :, :, fsl].rearrange("t p f -> p t f")),
                    inc=(qX, 16))

            plan.wait("PE", qX, z3l)
            a_last = None
            for mt in range(HT):
                k1, k2 = wsl[2 * (mt % 2)], wsl[2 * (mt % 2) + 1]
                r1 = stream_slab(k1, w1_d, mt)
                r2 = stream_slab(k2, w2_d, mt)
                plan.wait("PE", qW, r2)
                pvU = pvV = None
                for j in range(2):
                    bkU = next_bank()
                    bkU.acquire(plan)
                    for kt in range(DT):
                        pvU = PEo(lambda e, j=j, kt=kt, bk=bkU, w=k1: e.matmul(
                            bk.ap, w[:, ts(kt, P)],
                            z3[:, kt * FCH + j * 512: kt * FCH + (j + 1) * 512],
                            start=(kt == 0), stop=(kt == DT - 1)))
                    bkV = next_bank()
                    bkV.acquire(plan)
                    for kt in range(DT):
                        pvV = PEo(lambda e, j=j, kt=kt, bk=bkV, w=k2: e.matmul(
                            bk.ap, w[:, ts(kt, P)],
                            z3[:, kt * FCH + j * 512: kt * FCH + (j + 1) * 512],
                            start=(kt == 0), stop=(kt == DT - 1)),
                            inc=(cP, 1) if kt == DT - 1 else None)
                    uu = u_s[uflip[0] % 4]
                    vv = v2_s[uflip[0] % 4]
                    uflip[0] += 1
                    pvU = pvV
                    plan.wait("ACT", cP, pvV)
                    if id(uu) in dst_prev_users:
                        plan.wait("ACT", *dst_prev_users[id(uu)])
                    av = ACTo(lambda e, mt=mt, bk=bkU, uu=uu: e.activation(
                        uu[:, :], bk.ap, AF.Silu, scale=s1_s[:, mt:mt + 1]),
                        inc=(cA, 1))
                    bkU.free = (cA, av)
                    av = ACTo(lambda e, mt=mt, bk=bkV, vv=vv: e.activation(
                        vv[:, :], bk.ap, AF.Copy, scale=s2_s[:, mt:mt + 1]),
                        inc=(cA, 1))
                    bkV.free = (cA, av)
                    plan.wait("DVE", cA, av)
                    if mt == 0 and j == 0:
                        plan.wait("DVE", qS, qS.v)
                    a_last = DVEo(lambda e, mt=mt, j=j, uu=uu, vv=vv:
                                  e.tensor_mul(
                        abuf[:, mt * FCH + j * 512: mt * FCH + (j + 1) * 512],
                        uu[:, :], vv[:, :]), inc=(cD, 1))
                    dst_prev_users[id(uu)] = (cD, a_last)
                dst_prev_users[id(k1)] = (cP, pvU)
                dst_prev_users[id(k2)] = (cP, pvV)
            a_ready = a_last

            def h_stage(src_h, dst_h, h_dd, hs_col, rdy_sem, rdy_val):
                plan.wait("PE", rdy_sem, rdy_val)
                last_av = None
                pv = None
                for mt in range(HT):
                    slab = hsl[mt % 2]
                    rr = stream_slab(slab, h_dd, mt)
                    plan.wait("PE", qW, rr)
                    bks = []
                    for j in range(2):
                        bk = next_bank()
                        bk.acquire(plan)
                        bks.append(bk)
                        for kt in range(HT):
                            pv = PEo(lambda e, j=j, kt=kt, bk=bk, sh=src_h,
                                     slab=slab: e.matmul(
                                bk.ap, slab[:, ts(kt, P)],
                                sh[:, kt * FCH + j * 512:
                                   kt * FCH + (j + 1) * 512],
                                start=(kt == 0), stop=(kt == HT - 1)),
                                inc=(cP, 1)
                                if (kt == HT - 1 and j == 1) else None)
                    plan.wait("ACT", cP, pv)
                    for j in range(2):
                        last_av = ACTo(lambda e, mt=mt, j=j, bk=bks[j],
                                       dh=dst_h, hc=hs_col: e.activation(
                            dh[:, mt * FCH + j * 512: mt * FCH + (j + 1) * 512],
                            bk.ap, AF.Gelu, scale=hc[:, mt:mt + 1]),
                            inc=(cA, 1))
                        bks[j].free = (cA, last_av)
                    dst_prev_users[id(slab)] = (cP, pv)
                return last_av

            b_ready = h_stage(abuf, bbuf, h1_d, h1s_s, cD, a_ready)
            c_ready = h_stage(bbuf, abuf, h2_d, h2s_s, cA, b_ready)

            plan.wait("PE", cA, c_ready)
            fdone = None
            w3_pv = None
            for mt in range(DT):
                slab = hsl[mt % 2]
                rr = stream_slab(slab, w3_d, mt)
                plan.wait("PE", qW, rr)
                for j in range(2):
                    bk = next_bank()
                    bk.acquire(plan)
                    for kt in range(HT):
                        w3_pv = PEo(lambda e, j=j, kt=kt, bk=bk, slab=slab:
                                    e.matmul(
                            bk.ap, slab[:, ts(kt, P)],
                            abuf[:, kt * FCH + j * 512:
                                 kt * FCH + (j + 1) * 512],
                            start=(kt == 0), stop=(kt == HT - 1)),
                            inc=(cP, 1) if kt == HT - 1 else None)
                    plan.wait("ACT", cP, w3_pv)
                    fdone = ACTo(lambda e, mt=mt, j=j, bk=bk: e.activation(
                        fbuf[:, mt * FCH + j * 512: mt * FCH + (j + 1) * 512],
                        bk.ap, AF.Copy, scale=w3s_s[:, mt:mt + 1]),
                        inc=(cA, 1))
                    bk.free = (cA, fdone)
                dst_prev_users[id(slab)] = (cP, w3_pv)

            if fc + 1 < NFCH:
                plan.wait("SP", cP, cP.v)
                plan.wait("SP", qS, a2_end_qs)
                nfsl = slice((fc + 1) * FCH, (fc + 2) * FCH)
                z3l = SPo(lambda e, nfsl=nfsl, b=b: e.dma_start(
                    z3[:, :].rearrange("p (t f) -> p t f", t=DT),
                    z3d.ap()[b, :, :, nfsl].rearrange("t p f -> p t f")),
                    inc=(qX, 16))

            plan.wait("SP", cP, w3_pv)
            x2l2 = SPo(lambda e, fsl=fsl, b=b: e.dma_start(
                x2c2[:, :].rearrange("p (t f) -> p t f", t=DT),
                x2d.ap()[b, :, :, fsl].rearrange("t p f -> p t f")),
                inc=(qX, 16))

            plan.wait("ACT", cA, fdone)
            plan.wait("ACT", cD, a2_end_cd)
            plan.wait("ACT", qS, a2_end_qs)
            ACTo(lambda e: e.activation(fb16[:, :], fbuf[:, :], AF.Copy))
            cast_av = ACTo(lambda e: e.activation(fsq16[:, :], fbuf[:, :],
                                                  AF.Square), inc=(cA, 1))
            st4 = emit_ln_p1(FCH, fb16, fsq16, cast_av)
            if fc == NFCH - 1:
                prevF_ln4 = (cast_av, cP.v)
            plan.wait("DVE", qX, x2l2)

            def apply_res4(slz, Rap, Aap, last):
                DVEo(lambda e, slz=slz, Rap=Rap: e.tensor_mul(
                    t3[:, :], fbuf[:, slz], Rap))
                DVEo(lambda e, Aap=Aap: e.tensor_add(
                    t4[:, :], t3[:, :], Aap))
                return DVEo(lambda e, slz=slz: e.tensor_add(
                    x2c2[:, slz], x2c2[:, slz], t4[:, :]),
                    inc=(cD, 1) if last else None)

            res_dv = emit_ln_p2(FCH, st4, fbuf, None, apply_cb=apply_res4)
            plan.wait("SP", cD, res_dv)
            SPo(lambda e, fsl=fsl, b=b: e.dma_start(
                outd.ap()[b, :, :, fsl].rearrange("t p f -> p t f"),
                x2c2[:, :].rearrange("p (t f) -> p t f", t=DT)), inc=(qS, 16))

        attn_gate = [(cP, w3_pv)]
        prevF_cd = cD.v
        prevF_qs = qS.v

    plan.wait("SP", qS, qS.v)

    with nc.Block() as block:
        @block.tensor
        def _(eng):
            for f in plan.E["PE"]:
                f(eng)

        @block.scalar
        def _(eng):
            for f in plan.E["ACT"]:
                f(eng)

        @block.vector
        def _(eng):
            for f in plan.E["DVE"]:
                f(eng)

        @block.sync
        def _(eng):
            for f in plan.E["SP"]:
                f(eng)

        @block.gpsimd
        def _(eng):
            for f in plan.E["PL"]:
                f(eng)

    return nc



def _prep_weights(inp):
    import ml_dtypes as mld
    f32 = np.float32

    def lhsT_pack(w_din_dout, dt_in, dt_out):
        w = np.ascontiguousarray(np.asarray(w_din_dout, f32))
        t = w.reshape(dt_in, P, dt_out, P).transpose(1, 2, 0, 3)
        return np.ascontiguousarray(t.reshape(P, dt_out * dt_in * P))

    def slab_pack(w_din_dout, dt_in, dt_out):
        w = np.ascontiguousarray(np.asarray(w_din_dout, f32))
        t = w.reshape(dt_in, P, dt_out, P).transpose(2, 1, 0, 3)
        return np.ascontiguousarray(t.reshape(dt_out, P, dt_in * P))

    def col(v, nt):
        return np.ascontiguousarray(np.asarray(v, f32).reshape(nt, P).T)

    mask = np.zeros((P, NACH * ACH), dtype=f32)
    i = np.arange(P)[:, None]
    j = np.arange(ACH)[None, :]
    for v in range(NACH):
        mask[:, v * ACH:(v + 1) * ACH] = np.where(j > i + P * v, MASKVAL, 0.0)

    return {
        "wq": lhsT_pack(inp["L0_attn_Wq"], DT, DT),
        "wk": lhsT_pack(inp["L0_attn_Wk"], DT, DT),
        "wo": lhsT_pack(inp["L0_attn_Wo"], DT, DT),
        "wv": np.ascontiguousarray(
            np.asarray(inp["L0_attn_Wv"], f32).reshape(DT, P, D)
            .transpose(1, 0, 2).reshape(P, DT * D)),
        "w1p": slab_pack(np.asarray(inp["L0_ffn_w1_w"], f32).T, DT, HT),
        "w2p": slab_pack(np.asarray(inp["L0_ffn_w2_w"], f32).T, DT, HT),
        "h1p": slab_pack(np.asarray(inp["L0_ffn_h1_w"], f32).T, HT, HT),
        "h2p": slab_pack(np.asarray(inp["L0_ffn_h2_w"], f32).T, HT, HT),
        "w3p": slab_pack(np.asarray(inp["L0_ffn_w3_w"], f32).T, HT, DT),
        "bq": col(inp["L0_attn_bq"], DT),
        "bk": col(inp["L0_attn_bk"], DT),
        "bo": col(inp["L0_attn_bo"], DT),
        "bvb": np.ascontiguousarray(
            np.asarray(inp["L0_attn_bv"], f32).reshape(1, D)),
        "s1c": col(inp["L0_ffn_w1_s"], HT),
        "s2c": col(inp["L0_ffn_w2_s"], HT),
        "h1sc": col(inp["L0_ffn_h1_s"], HT),
        "h2sc": col(inp["L0_ffn_h2_s"], HT),
        "w3sc": col(inp["L0_ffn_w3_s"], DT),
        "mask": mask,
        "iden": np.eye(P, dtype=mld.bfloat16),
        "onesc": np.ones((P, 1), dtype=mld.bfloat16),
        "onesr": np.ones((1, P), dtype=np.float32),
    }


def kernel(**inputs):
    x = np.asarray(inputs["L0_attn_x"], dtype=np.float32)
    shared = _prep_weights(inputs)
    nc = build_program()
    in_maps = []
    for c in range(NCORES):
        m = dict(shared)
        m["xT"] = np.ascontiguousarray(
            x[c * NB:(c + 1) * NB].transpose(0, 2, 1)).reshape(NB, DT, P, SEQ)
        in_maps.append(m)
    res = run_bass_kernel_spmd(nc, in_maps, core_ids=list(range(NCORES)))
    out = np.empty((NCORES * NB, SEQ, D), dtype=np.float32)
    for c in range(NCORES):
        y = np.asarray(res.results[c]["outT"])
        out[c * NB:(c + 1) * NB] = y.reshape(NB, D, SEQ).transpose(0, 2, 1)
    return out
